# version 02 (final)
# baseline (speedup 1.0000x reference)
import math
import numpy as np
import ml_dtypes
import concourse.bass as bass
import concourse.mybir as mybir
from concourse.bass_utils import run_bass_kernel_spmd

F32 = mybir.dt.float32
BF16 = mybir.dt.bfloat16
AF = mybir.ActivationFunctionType
ALU = mybir.AluOpType
AX = mybir.AxisListType
BF = ml_dtypes.bfloat16

NC = 8
T = 2048
D = 1024

ENGINES = ("sync", "gpsimd", "vector", "scalar", "tensor")


class Prog:

    def __init__(self, name="prog"):
        self.nc = bass.Bass("TRN2", target_bir_lowering=False, debug=False,
                            detect_race_conditions=False)
        self.ops = {e: [] for e in ENGINES}
        self.sem_handles = {}
        self.sem_plan = {}
        self.wait_cache = {e: {} for e in ENGINES}
        self._names = {}
        self.psum_last_read = [None] * 8
        self.psum_banks = []
        self.ones_row128 = None
        self.ones_col128 = None

    def name(self, base):
        i = self._names.get(base, 0)
        self._names[base] = i + 1
        return f"{base}_{i}" if i else base

    def sem(self, name):
        if name not in self.sem_handles:
            self.sem_handles[name] = self.nc.alloc_semaphore(name)
            self.sem_plan[name] = 0
        return self.sem_handles[name]

    def sbuf(self, name, shape, dtype):
        return self.nc.alloc_sbuf_tensor(self.name(name), shape, dtype)

    def dram(self, name, shape, dtype, kind="Internal"):
        return self.nc.dram_tensor(name, shape, dtype, kind=kind)

    def psum_init(self):
        for b in range(8):
            self.psum_banks.append(
                self.nc.alloc_psum_tensor(f"pb{b}", [128, 512], F32))

    def pbf(self, bank):
        return self.psum_banks[bank].ap().bitcast(BF16)

    def init_ones(self):
        self.ones_col128 = self.sbuf("onescol", [128, 1], F32)
        self.ones_row128 = self.sbuf("onesrow", [1, 128], F32)
        self.ones_col_bf = self.sbuf("onescolb", [128, 1], BF16)
        self.ones_row_bf = self.sbuf("onesrowb", [1, 128], BF16)
        t0 = self.op("vector", lambda e: e.memset(self.ones_col128[:, :], 1.0),
                     inc="init")
        t1 = self.op("vector", lambda e: e.memset(self.ones_row128[:, :], 1.0),
                     inc="init")
        t2 = self.op("vector", lambda e: e.memset(self.ones_col_bf[:, :], 1.0),
                     inc="init")
        t3 = self.op("vector", lambda e: e.memset(self.ones_row_bf[:, :], 1.0),
                     inc="init")
        return [t0, t1, t2, t3]

    def op(self, eng, fn, after=(), inc=None, inc_amt=1):
        waits = []
        for tok in after:
            if tok is None:
                continue
            sname, val = tok
            cache = self.wait_cache[eng]
            if cache.get(sname, -1) < val:
                cache[sname] = val
                waits.append((self.sem(sname), val))
        tok_out = None
        upd = None
        if inc is not None:
            h = self.sem(inc)
            self.sem_plan[inc] += inc_amt
            tok_out = (inc, self.sem_plan[inc])
            upd = (h, inc_amt)

        def closure(e, waits=waits, fn=fn, upd=upd):
            for (h, val) in waits:
                e.wait_ge(h, val)
            ins = fn(e)
            if upd is not None and ins is not None:
                ins.then_inc(upd[0], upd[1])
            elif upd is not None:
                e.nop().then_inc(upd[0], upd[1])
        self.ops[eng].append(closure)
        return tok_out

    def barrier_wait(self, eng, after):
        self.op(eng, lambda e: None, after=after)

    def dma(self, eng, out_ap, in_ap, after=(), inc=None, **kw):
        assert inc is not None
        return self.op(eng, lambda e: e.dma_start(out=out_ap, in_=in_ap, **kw),
                       after=after, inc=inc, inc_amt=16)

    def build(self):
        nc = self.nc
        nums = sorted(h.num for h in self.sem_handles.values())
        if nums:
            assert nums[-1] - nums[0] + 1 >= len(nums)
            nc.gpsimd.sem_clear(range(nums[0], nums[-1] + 1))
        with nc.Block() as block:
            @block.sync
            def _(e):
                for f in self.ops["sync"]:
                    f(e)

            @block.gpsimd
            def _(e):
                for f in self.ops["gpsimd"]:
                    f(e)

            @block.vector
            def _(e):
                for f in self.ops["vector"]:
                    f(e)

            @block.scalar
            def _(e):
                for f in self.ops["scalar"]:
                    f(e)

            @block.tensor
            def _(e):
                for f in self.ops["tensor"]:
                    f(e)
        return nc


def ceil_div(a, b):
    return (a + b - 1) // b


class Carve:

    def __init__(self, p, total_f32, name="SCR"):
        self.t = p.sbuf(name, [128, total_f32], F32)
        self.total = total_f32
        self.off = 0

    def reset(self):
        self.off = 0

    def f32(self, n):
        ap = self.t[:, self.off:self.off + n]
        self.off += n
        assert self.off <= self.total, (self.off, self.total)
        return ap

    def bf16(self, n):
        n2 = (n + 1) // 2 * 2
        ap = self.t.ap().bitcast(BF16)[:, 2 * self.off: 2 * self.off + n]
        self.off += n2 // 2
        assert self.off <= self.total, (self.off, self.total)
        return ap


def fm_ap(dram_handle):
    return dram_handle.ap().rearrange("(c p) t -> p c t", p=128)


def fm_sb(sb, nchunks, ntok):
    ap = sb.ap() if hasattr(sb, "ap") and not hasattr(sb, "rearrange") else (
        sb if not hasattr(sb, "ap") else sb)
    try:
        return ap.rearrange("p (c t) -> p c t", c=nchunks)
    except AttributeError:
        return sb.ap().rearrange("p (c t) -> p c t", c=nchunks)



def emit_ln_fm(p, xt_sb, nchunks, ntok, x_off, out_bf, out_off, sq, tok_in,
               name="ln", eps=1e-5, out_ntok=None, rstd_out=None):
    work = ntok if out_ntok is None else out_ntok
    D_full = nchunks * 128
    stats = p.sbuf(name + "_stats", [1, 3 * 512], F32)
    tmp = p.sbuf(name + "_t", [128, 512], F32)
    ntt = ceil_div(work, 512)
    done_tok = None
    for tt in range(ntt):
        t0 = tt * 512
        tw = min(512, work - t0)
        ps_s, ps_q, ps_b = p.psum_banks[5], p.psum_banks[6], p.psum_banks[7]
        sq_toks = []
        for c in range(nchunks):
            src = xt_sb[:, c * ntok + x_off + t0: c * ntok + x_off + t0 + tw]
            tq = p.op("scalar", lambda e, src=src, c=c, tw=tw: e.activation(
                sq[:, c * 512:c * 512 + tw], src, AF.Square),
                after=[tok_in], inc="sq" + name)
            sq_toks.append(tq)
        mm_tok = None
        for c in range(nchunks):
            src = xt_sb[:, c * ntok + x_off + t0: c * ntok + x_off + t0 + tw]
            p.op("tensor", lambda e, src=src, c=c, tw=tw, ps_s=ps_s: e.matmul(
                ps_s[0:1, 0:tw], p.ones_col128[:, 0:1], src,
                start=(c == 0), stop=(c == nchunks - 1)),
                after=[tok_in, p.psum_last_read[5]] if c == 0 else ())
            mm_tok = p.op("tensor", lambda e, c=c, tw=tw, ps_q=ps_q: e.matmul(
                ps_q[0:1, 0:tw], p.ones_col128[:, 0:1],
                sq[:, c * 512:c * 512 + tw],
                start=(c == 0), stop=(c == nchunks - 1)),
                after=[sq_toks[c]] + ([p.psum_last_read[6]] if c == 0 else []),
                inc="mm" + name if c == nchunks - 1 else None)
        mu = stats[0:1, 0:tw]
        mr = stats[0:1, 512:512 + tw]
        rstd = stats[0:1, 1024:1024 + tw]
        w1 = p.op("vector", lambda e, mu=mu, ps_s=ps_s, tw=tw: e.tensor_scalar(
            mu, ps_s[0:1, 0:tw], 1.0 / D_full, None, ALU.mult), after=[mm_tok],
            inc="vd" + name)
        w2 = p.op("vector", lambda e, rstd=rstd, ps_q=ps_q, tw=tw: e.tensor_scalar(
            rstd, ps_q[0:1, 0:tw], 1.0 / D_full, None, ALU.mult),
            after=[mm_tok], inc="vd" + name)
        w3 = p.op("vector", lambda e, mr=mr, mu=mu: e.tensor_mul(mr, mu, mu),
                  after=[w1], inc="vd" + name)
        w4 = p.op("vector", lambda e, rstd=rstd, mr=mr: e.tensor_sub(rstd, rstd, mr),
                  after=[w2, w3], inc="vd" + name)
        var_done = p.op("vector", lambda e, rstd=rstd: e.tensor_scalar(
            rstd, rstd, eps, None, ALU.add), after=[w4], inc="vd" + name)
        w5 = p.op("scalar", lambda e, rstd=rstd: e.activation(rstd, rstd, AF.Ln),
             after=[var_done], inc="rd" + name)
        rstd_done = p.op("scalar", lambda e, rstd=rstd: e.activation(
            rstd, rstd, AF.Exp, scale=-0.5), after=[w5], inc="rd" + name)
        mr_done = p.op("vector", lambda e, mr=mr, mu=mu, rstd=rstd:
                       e.tensor_mul(mr, mu, rstd), after=[rstd_done],
                       inc="vd" + name)
        bc_r = p.op("tensor", lambda e, ps_b=ps_b, rstd=rstd, tw=tw: e.matmul(
            ps_b[:, 0:tw], p.ones_row128[0:1, :], rstd, start=True, stop=True),
            after=[rstd_done, p.psum_last_read[7]])
        bc_m = p.op("tensor", lambda e, ps_s=ps_s, mr=mr, tw=tw: e.matmul(
            ps_s[:, 0:tw], p.ones_row128[0:1, :], mr, start=True, stop=True),
            after=[mr_done], inc="mm" + name)
        last = None
        for c in range(nchunks):
            src = xt_sb[:, c * ntok + x_off + t0: c * ntok + x_off + t0 + tw]
            dst = out_bf[:, c * (out_ntok or ntok) + out_off + t0:
                         c * (out_ntok or ntok) + out_off + t0 + tw]
            p.op("vector", lambda e, src=src, tmp=tmp, tw=tw, ps_s=ps_s:
                 e.tensor_sub(tmp[:, 0:tw], src, ps_s[:, 0:tw]),
                 after=[bc_m, tok_in])
            last = p.op("vector", lambda e, dst=dst, tmp=tmp, tw=tw, ps_b=ps_b:
                        e.tensor_mul(dst, tmp[:, 0:tw], ps_b[:, 0:tw]),
                        after=[bc_r], inc="ev" + name)
        if rstd_out is not None:
            p.op("vector", lambda e, rstd=rstd, t0=t0, tw=tw: e.tensor_copy(
                rstd_out[0:1, t0:t0 + tw], rstd), after=[rstd_done],
                inc="vd" + name)
        p.psum_last_read[5] = last
        p.psum_last_read[6] = var_done
        p.psum_last_read[7] = last
        done_tok = last
    return done_tok



def build_L1():
    p = Prog("L1")
    xT = p.dram("xT", [D, T], F32, kind="ExternalInput")
    Wqk = p.dram("Wqk", [D, 2048], F32, kind="ExternalInput")
    Wv = p.dram("Wv", [D, D], F32, kind="ExternalInput")
    qT_o = p.dram("qT", [D, T], BF16, kind="ExternalOutput")
    kT_o = p.dram("kT", [D, T], BF16, kind="ExternalOutput")
    vtok_o = p.dram("vtok", [T, D], BF16, kind="ExternalOutput")
    ql_o = p.dram("qlkl", [D, 16], F32, kind="ExternalOutput")

    p.psum_init()
    xt = p.sbuf("xt", [128, 8 * T], F32)
    xhat = p.sbuf("xhat", [128, 8 * T], BF16)
    wqk = p.sbuf("wqk", [128, 8 * 2048], BF16)
    wv = p.sbuf("wv", [128, 8 * 1024], BF16)
    stage = p.sbuf("stage", [128, 4 * 2048], BF16)
    sq = p.sbuf("sq", [128, 8 * 512], F32)
    qlkl = p.sbuf("qlkl_sb", [128, 8 * 16], F32)
    init_toks = p.init_ones()

    tok_x = p.dma("sync", fm_sb(xt, 8, T), fm_ap(xT), inc="ld")
    tok_wqk = p.dma("gpsimd", fm_sb(wqk, 8, 2048), fm_ap(Wqk), inc="ldw")
    tok_wv = p.dma("gpsimd", fm_sb(wv, 8, 1024), fm_ap(Wv), inc="ldw")

    ln_tok = emit_ln_fm(p, xt, 8, T, 0, xhat, 0, sq, tok_x, name="ln1")
    ready = [ln_tok, tok_wqk, tok_wv] + init_toks

    stage_free = [None, None, None, None]
    store_toks = []
    for oc in range(16):
        slot = oc % 4
        ev_toks = []
        for tt in range(4):
            n0 = tt * 512
            bank = tt % 4
            ps = p.psum_banks[bank]
            mm_tok = None
            for kc in range(8):
                mm_tok = p.op("tensor", lambda e, kc=kc, oc=oc, n0=n0, ps=ps:
                              e.matmul(ps[:, 0:512],
                                       wqk[:, kc * 2048 + oc * 128:
                                           kc * 2048 + oc * 128 + 128],
                                       xhat[:, kc * T + n0:kc * T + n0 + 512],
                                       start=(kc == 0), stop=(kc == 7)),
                              after=(ready + [p.psum_last_read[bank]])
                              if kc == 0 else (), inc="mm" if kc == 7 else None)
            ev = p.op("vector", lambda e, slot=slot, n0=n0, ps=ps:
                      e.tensor_copy(stage[:, slot * 2048 + n0:
                                          slot * 2048 + n0 + 512],
                                    ps[:, 0:512]),
                      after=[mm_tok, stage_free[slot]], inc="ev")
            p.psum_last_read[bank] = ev
            ev_toks.append(ev)
        cc = oc if oc < 8 else oc - 8
        lcol = cc * 16 + (0 if oc < 8 else 8)
        lm = p.op("vector", lambda e, lcol=lcol, slot=slot: e.reduce_sum(
            qlkl[:, lcol:lcol + 8],
            stage.ap().rearrange("p (sl s f) -> p sl s f", sl=4, s=8)[:, slot],
            axis=AX.X), after=ev_toks, inc="lm")
        dst = qT_o if oc < 8 else kT_o
        row0 = (oc % 8) * 128
        st = p.dma("sync", dst.ap()[row0:row0 + 128, :],
                   stage[:, slot * 2048:(slot + 1) * 2048],
                   after=ev_toks, inc="st")
        stage_free[slot] = st
        store_toks.append(st)
        if oc == 15:
            lm_last = lm
    lm_tok = p.op("vector", lambda e: e.tensor_scalar(
        qlkl[:, :], qlkl[:, :], 1.0 / 256.0, None, ALU.mult),
        after=[lm_last], inc="lm")
    st4 = p.dma("sync", fm_ap(ql_o), fm_sb(qlkl, 8, 16), after=[lm_tok], inc="st")
    store_toks.append(st4)

    vstage = p.sbuf("vstage", [128, 2 * 1024], BF16)
    vfree = [None, None]
    for tc in range(16):
        slot = tc % 2
        ev_toks = []
        for tt in range(2):
            n0 = tt * 512
            bank = 4 + (tt % 2)
            ps = p.psum_banks[bank]
            mm_tok = None
            for kc in range(8):
                mm_tok = p.op("tensor", lambda e, kc=kc, tc=tc, n0=n0, ps=ps:
                              e.matmul(ps[:, 0:512],
                                       xhat[:, kc * T + tc * 128:
                                            kc * T + tc * 128 + 128],
                                       wv[:, kc * 1024 + n0:kc * 1024 + n0 + 512],
                                       start=(kc == 0), stop=(kc == 7)),
                              after=(ready + [p.psum_last_read[bank]])
                              if kc == 0 else (), inc="mm" if kc == 7 else None)
            ev = p.op("scalar", lambda e, slot=slot, n0=n0, ps=ps:
                      e.copy(vstage[:, slot * 1024 + n0:slot * 1024 + n0 + 512],
                             ps[:, 0:512]),
                      after=[mm_tok, vfree[slot]], inc="ev")
            p.psum_last_read[bank] = ev
            ev_toks.append(ev)
        st = p.dma("sync", vtok_o.ap()[tc * 128:(tc + 1) * 128, :],
                   vstage[:, slot * 1024:(slot + 1) * 1024],
                   after=ev_toks, inc="st")
        vfree[slot] = st
        store_toks.append(st)

    p.barrier_wait("sync", store_toks)
    return p.build()



def build_L2():
    p = Prog("L2")
    kT_i = p.dram("kT", [D, T], BF16, kind="ExternalInput")
    vtok_i = p.dram("vtok", [T, D], BF16, kind="ExternalInput")
    ql_i = p.dram("ql_all", [D, 64], F32, kind="ExternalInput")
    nv_o = p.dram("nv", [64, 16 * 65], F32, kind="ExternalOutput")

    p.psum_init()
    kT = p.sbuf("kT_sb", [128, 8 * T], BF16)
    vtok = p.sbuf("vtok_sb", [128, 16 * 1024], BF16)
    ql = p.sbuf("ql_sb", [128, 8 * 64], BF16)
    Es = [p.sbuf("E_sb", [128, 64], BF16) for _ in range(4)]
    nv = p.sbuf("nv_sb", [64, 16 * 65], F32)
    init_toks = p.init_ones()

    tk = p.dma("sync", fm_sb(kT, 8, T), fm_ap(kT_i), inc="ld")
    tv = p.dma("sync", vtok.ap().rearrange("p (c o) -> p c o", c=16),
               vtok_i.ap().rearrange("(c p) o -> p c o", p=128), inc="ld")
    tq = p.dma("gpsimd", fm_sb(ql, 8, 64), fm_ap(ql_i), inc="ldw")
    ready = [tk, tv, tq] + init_toks

    e_free = [None] * 4
    for h in range(16):
        ch, bp = h // 2, (h % 2) * 64
        nvbank = 4 + (h % 2)
        ev_prev = p.psum_last_read[nvbank]
        last_mm = None
        for tc in range(16):
            eb = Es[tc % 4]
            bank = tc % 2
            ps = p.psum_banks[bank]
            mm = p.op("tensor", lambda e, ch=ch, bp=bp, tc=tc, ps=ps: e.matmul(
                ps[0:128, 0:64],
                kT[bp:bp + 64, ch * T + tc * 128: ch * T + tc * 128 + 128],
                ql[bp:bp + 64, ch * 64:(ch + 1) * 64],
                start=True, stop=True),
                after=(ready if (h == 0 and tc == 0) else []) +
                      [p.psum_last_read[bank]],
                inc="mmE")
            ex = p.op("scalar", lambda e, eb=eb, ps=ps: e.activation(
                eb[:, :], ps[0:128, 0:64], AF.Exp),
                after=[mm, e_free[tc % 4]], inc="exp")
            p.psum_last_read[bank] = ex
            psnv = p.psum_banks[nvbank]
            p.op("tensor", lambda e, eb=eb, h=h, tc=tc, psnv=psnv: e.matmul(
                psnv[0:64, 0:64], eb[:, :],
                vtok[:, tc * 1024 + h * 64: tc * 1024 + h * 64 + 64],
                start=(tc == 0), stop=False),
                after=[ex] + ([ev_prev] if tc == 0 else []))
            last_mm = p.op("tensor", lambda e, eb=eb, tc=tc, psnv=psnv: e.matmul(
                psnv[0:64, 64:65], eb[:, :], p.ones_col_bf[:, 0:1],
                start=False, stop=(tc == 15)),
                inc="mmNV" if tc == 15 else None)
            e_free[tc % 4] = last_mm
        ev = p.op("vector", lambda e, h=h, psnv=psnv: e.tensor_copy(
            nv[:, h * 65:(h + 1) * 65], psnv[0:64, 0:65]),
            after=[last_mm], inc="evNV")
        p.psum_last_read[nvbank] = ev

    st = p.dma("sync", nv_o.ap(), nv[:, :], after=[("evNV", 16)], inc="st")
    p.barrier_wait("sync", [st])
    return p.build()



def build_L3():
    p = Prog("L3")
    qT_i = p.dram("qT", [D, T], BF16, kind="ExternalInput")
    ql_i = p.dram("ql_all", [D, 64], F32, kind="ExternalInput")
    kl_i = p.dram("kl_all", [D, 64], F32, kind="ExternalInput")
    nv_i = p.dram("nv_all", [64, 8 * 16 * 65], F32, kind="ExternalInput")
    Wp_i = p.dram("Wproj", [D, D], F32, kind="ExternalInput")
    xT_i = p.dram("xT", [D, T], F32, kind="ExternalInput")
    xa_o = p.dram("xa", [D, T], F32, kind="ExternalOutput")
    P_o = p.dram("P_dbg", [64, 16 * 64], F32, kind="ExternalOutput")

    p.psum_init()
    qT = p.sbuf("qT_sb", [128, 8 * T], BF16)
    ql = p.sbuf("ql_sb", [128, 8 * 64], BF16)
    kl = p.sbuf("kl_sb", [128, 8 * 64], BF16)
    nva = p.sbuf("nv_sb", [64, 8 * 1040], F32)
    wp = p.sbuf("wp_sb", [128, 8 * 1024], BF16)
    xt = p.sbuf("xt_sb", [128, 8 * 512], F32)
    nvn = p.sbuf("nvn_sb", [64, 16 * 64], F32)
    Z = p.sbuf("Z_sb", [64, 16 * 64], BF16)
    Kp = p.sbuf("Kp_sb", [64, 16 * 64], F32)
    KpT = p.sbuf("KpT_sb", [64, 16 * 64], F32)
    Vit = p.sbuf("Vit_sb", [64, 16 * 64], F32)
    VT = p.sbuf("VT_sb", [64, 16 * 64], F32)
    KV = p.sbuf("KV_sb", [64, 16 * 64], F32)
    KVT = p.sbuf("KVT_sb", [64, 16 * 64], F32)
    T1 = p.sbuf("T1_sb", [64, 16 * 64], F32)
    T3 = p.sbuf("T3_sb", [64, 16 * 64], F32)
    eye = p.sbuf("eye_sb", [64, 64], F32)
    scr = p.sbuf("scr_sb", [64, 16 * 2], F32)
    EL = p.sbuf("EL_sb", [64, 512], BF16)
    rb = p.sbuf("rb_sb", [64, 512], F32)
    yT = p.sbuf("yT_sb", [128, 8 * 512], BF16)
    row1 = p.sbuf("row1_sb", [1, 512], F32)
    init_toks = p.init_ones()

    tq = p.dma("sync", fm_sb(qT, 8, T), fm_ap(qT_i), inc="ld")
    tql = p.dma("gpsimd", fm_sb(ql, 8, 64), fm_ap(ql_i), inc="ldw")
    tkl = p.dma("gpsimd", fm_sb(kl, 8, 64), fm_ap(kl_i), inc="ldw")
    tnv = p.dma("sync", nva[:, :], nv_i.ap(), inc="ld")
    twp = p.dma("gpsimd", fm_sb(wp, 8, 1024), fm_ap(Wp_i), inc="ldw")

    id_i = p.dram("ident", [64, 64], F32, kind="ExternalInput")
    tid = p.dma("sync", eye[:, :], id_i.ap(), inc="ld")
    eye_bf = p.sbuf("eyebf_sb", [64, 64], BF16)
    tid2 = p.op("vector", lambda e: e.tensor_copy(eye_bf[:, :], eye[:, :]),
                after=[tid], inc="init")
    p.eye7 = p.sbuf("eye7_sb", [64, 64], F32)
    p.eye15 = p.sbuf("eye15_sb", [64, 64], F32)
    p.eye13 = p.sbuf("eye13_sb", [64, 64], F32)
    Pdbg_sb = p.sbuf("Pdbg_sb", [64, 16 * 64], F32)
    e7 = p.op("vector", lambda e: e.tensor_scalar(
        p.eye7[:, :], eye[:, :], 7.0, None, ALU.mult), after=[tid], inc="init")
    e15 = p.op("vector", lambda e: e.tensor_scalar(
        p.eye15[:, :], eye[:, :], 15.0, None, ALU.mult), after=[tid], inc="init")
    e13 = p.op("vector", lambda e: e.tensor_scalar(
        p.eye13[:, :], eye[:, :], 13.0, None, ALU.mult), after=[tid], inc="init")
    ready0 = init_toks + [tql, tkl, tnv, tid, tid2, e7, e15, e13]

    nv_sum = None
    for c in range(1, 8):
        nv_sum = p.op("vector", lambda e, c=c: e.tensor_add(
            nva[:, 0:1040], nva[:, 0:1040],
            nva[:, c * 1040:(c + 1) * 1040]), after=[tnv],
            inc="nvsum")

    nvn_toks = []
    for h in range(16):
        rc = p.op("vector", lambda e, h=h: e.reciprocal(
            scr[:, 2 * h:2 * h + 1], nva[:, h * 65 + 64:h * 65 + 65]),
            after=[nv_sum], inc="nvn")
        tkn = p.op("vector", lambda e, h=h: e.tensor_scalar(
            nvn[:, h * 64:(h + 1) * 64], nva[:, h * 65:h * 65 + 64],
            scr[:, 2 * h:2 * h + 1], None, ALU.mult), after=[rc], inc="nvn")
        nvn_toks.append(tkn)

    def head_mm(h, out_ps, lhsT, rhs, start=True, stop=True, after=(), inc=None):
        return p.op("tensor", lambda e: e.matmul(out_ps, lhsT, rhs,
                    start=start, stop=stop), after=after, inc=inc)

    P_toks = []
    for h in range(16):
        ch, bp = h // 2, (h % 2) * 64
        sl = slice(h * 64, (h + 1) * 64)
        ps = p.psum_banks[h % 2]
        ps2 = p.psum_banks[2 + h % 2]
        t_mm = p.op("tensor", lambda e, ch=ch, bp=bp, ps=ps: e.matmul(
            ps[0:64, 0:64], ql[bp:bp + 64, ch * 64:(ch + 1) * 64],
            kl[bp:bp + 64, ch * 64:(ch + 1) * 64], start=True, stop=True),
            after=ready0 + [p.psum_last_read[h % 2]], inc="pm")
        mx = scr[:, 2 * h:2 * h + 1]
        t_mx = p.op("vector", lambda e, mx=mx, ps=ps: e.reduce_max(
            mx, ps[0:64, 0:64], axis=AX.X, negate=True), after=[t_mm], inc="pv")
        t_ex = p.op("scalar", lambda e, ps=ps, mx=mx, sl=sl: e.activation(
            Kp[:, sl], ps[0:64, 0:64], AF.Exp, bias=mx), after=[t_mx], inc="pe")
        p.psum_last_read[h % 2] = t_ex
        sm = scr[:, 2 * h + 1:2 * h + 2]
        t_sm = p.op("vector", lambda e, sm=sm, sl=sl: e.reduce_sum(
            sm, Kp[:, sl], axis=AX.X), after=[t_ex], inc="pv")
        t_rc = p.op("vector", lambda e, sm=sm: e.reciprocal(sm, sm),
                    after=[t_sm], inc="pv")
        t_kp = p.op("vector", lambda e, sl=sl, sm=sm: e.tensor_scalar(
            Kp[:, sl], Kp[:, sl], sm, None, ALU.mult), after=[t_rc], inc="pv")
        t_cs = p.op("tensor", lambda e, sl=sl, ps2=ps2: e.matmul(
            ps2[0:1, 0:64], p.ones_col128[0:64, 0:1], Kp[:, sl],
            start=True, stop=True),
            after=[t_kp, p.psum_last_read[2 + h % 2]], inc="pm")
        t_dn = p.op("vector", lambda e, ps2=ps2: e.reduce_max(
            row1[0:1, 0:1], ps2[0:1, 0:64], axis=AX.X), after=[t_cs], inc="pv")
        t_dr = p.op("vector", lambda e: e.reciprocal(row1[0:1, 0:1],
                    row1[0:1, 0:1]), after=[t_dn], inc="pv")
        t_db = p.op("tensor", lambda e, ps2=ps2: e.matmul(
            ps2[0:64, 64:65], p.ones_row128[0:1, 0:64], row1[0:1, 0:1],
            start=True, stop=True), after=[t_dr], inc="pm")
        dcol = scr[:, 2 * h:2 * h + 1]
        t_dc = p.op("vector", lambda e, dcol=dcol, ps2=ps2: e.tensor_copy(
            dcol, ps2[0:64, 64:65]), after=[t_db], inc="pv")
        p.psum_last_read[2 + h % 2] = t_dc
        pb2 = 2 + h % 2
        t_tr = p.op("tensor", lambda e, sl=sl, ps2=ps2: e.transpose(
            ps2[0:64, 0:64], Kp[:, sl], eye[:, :]),
            after=[t_kp, t_dc], inc="pm")
        t_v0 = p.op("vector", lambda e, sl=sl, ps2=ps2, dcol=dcol: e.tensor_scalar(
            Vit[:, sl], ps2[0:64, 0:64], dcol, None, ALU.mult),
            after=[t_tr], inc="pv")
        t_kpt = p.op("vector", lambda e, sl=sl, ps2=ps2: e.tensor_copy(
            KpT[:, sl], ps2[0:64, 0:64]), after=[t_tr], inc="pv")
        p.psum_last_read[2 + h % 2] = t_kpt
        last = [t_v0, t_kpt]
        for it in range(6):
            t1 = p.op("tensor", lambda e, sl=sl, ps=ps: e.matmul(
                ps[0:64, 0:64], KpT[:, sl], Vit[:, sl], start=True, stop=True),
                after=last + [p.psum_last_read[h % 2]], inc="pm")
            t2 = p.op("vector", lambda e, sl=sl, ps=ps: e.tensor_copy(
                KV[:, sl], ps[0:64, 0:64]), after=[t1], inc="pv")
            t3 = p.op("vector", lambda e, sl=sl, ps=ps: e.scalar_tensor_tensor(
                T1[:, sl], ps[0:64, 0:64], -1.0, p.eye7[:, :],
                ALU.mult, ALU.add), after=[t1], inc="pv")
            p.psum_last_read[h % 2] = t3
            t4 = p.op("tensor", lambda e, sl=sl, ps2=ps2: e.transpose(
                ps2[0:64, 0:64], KV[:, sl], eye[:, :]),
                after=[t2, p.psum_last_read[2 + h % 2]], inc="pm")
            t5 = p.op("vector", lambda e, sl=sl, ps2=ps2: e.tensor_copy(
                KVT[:, sl], ps2[0:64, 0:64]), after=[t4], inc="pv")
            p.psum_last_read[2 + h % 2] = t5
            t6 = p.op("tensor", lambda e, sl=sl, ps=ps: e.matmul(
                ps[0:64, 0:64], KVT[:, sl], T1[:, sl], start=True, stop=True),
                after=[t3, t5], inc="pm")
            t7 = p.op("vector", lambda e, sl=sl, ps=ps: e.scalar_tensor_tensor(
                T3[:, sl], ps[0:64, 0:64], -1.0, p.eye15[:, :],
                ALU.mult, ALU.add), after=[t6], inc="pv")
            p.psum_last_read[h % 2] = t7
            t8 = p.op("tensor", lambda e, sl=sl, ps=ps: e.matmul(
                ps[0:64, 0:64], KVT[:, sl], T3[:, sl], start=True, stop=True),
                after=[t7], inc="pm")
            t9 = p.op("vector", lambda e, sl=sl, ps=ps: e.scalar_tensor_tensor(
                T1[:, sl], ps[0:64, 0:64], -1.0, p.eye13[:, :],
                ALU.mult, ALU.add), after=[t8], inc="pv")
            p.psum_last_read[h % 2] = t9
            t10 = p.op("tensor", lambda e, sl=sl, ps2=ps2: e.transpose(
                ps2[0:64, 0:64], Vit[:, sl], eye[:, :]),
                after=[t9, p.psum_last_read[2 + h % 2]], inc="pm")
            t11 = p.op("vector", lambda e, sl=sl, ps2=ps2: e.tensor_copy(
                VT[:, sl], ps2[0:64, 0:64]), after=[t10], inc="pv")
            p.psum_last_read[2 + h % 2] = t11
            t12 = p.op("tensor", lambda e, sl=sl, ps=ps: e.matmul(
                ps[0:64, 0:64], VT[:, sl], T1[:, sl], start=True, stop=True),
                after=[t11, t9], inc="pm")
            t13 = p.op("vector", lambda e, sl=sl, ps=ps: e.tensor_scalar(
                Vit[:, sl], ps[0:64, 0:64], 0.25, None, ALU.mult),
                after=[t12], inc="pv")
            p.psum_last_read[h % 2] = t13
            last = [t13]
        tz1 = p.op("tensor", lambda e, sl=sl, ps2=ps2: e.transpose(
            ps2[0:64, 0:64], Vit[:, sl], eye[:, :]),
            after=last + [p.psum_last_read[2 + h % 2]], inc="pm")
        tz2 = p.op("vector", lambda e, sl=sl, ps2=ps2: e.tensor_copy(
            VT[:, sl], ps2[0:64, 0:64]), after=[tz1], inc="pv")
        p.psum_last_read[2 + h % 2] = tz2
        tz3 = p.op("tensor", lambda e, sl=sl, ps=ps: e.matmul(
            ps[0:64, 0:64], VT[:, sl], nvn[:, sl], start=True, stop=True),
            after=[tz2, nvn_toks[h], p.psum_last_read[h % 2]], inc="pm")
        tz4 = p.op("vector", lambda e, sl=sl, ps=ps: e.tensor_copy(
            Z[:, sl], ps[0:64, 0:64]), after=[tz3], inc="pv")
        tz5 = p.op("vector", lambda e, sl=sl: e.tensor_copy(
            Pdbg_sb[:, sl], Vit[:, sl]), after=last, inc="pv")
        p.psum_last_read[h % 2] = tz4
        P_toks.append(tz4)
        P_toks.append(tz5)

    xa_stage = p.sbuf("xa_stage", [128, 8 * 512], F32)
    st_toks = []
    xa_free = None
    xt_free = None
    for tt in range(4):
        t0 = tt * 512
        txt = p.dma("sync", fm_sb(xt, 8, 512),
                    xT_i.ap().rearrange("(c p) t -> p c t", p=128)
                    [:, :, t0:t0 + 512], after=[xt_free], inc="ld")
        y_toks = []
        for h in range(16):
            ch, bp = h // 2, (h % 2) * 64
            sl = slice(h * 64, (h + 1) * 64)
            ps = p.psum_banks[h % 4]
            ps_y = p.psum_banks[4 + h % 2]
            t_mm = p.op("tensor", lambda e, ch=ch, bp=bp, t0=t0, ps=ps: e.matmul(
                ps[0:64, 0:512], kl[bp:bp + 64, ch * 64:(ch + 1) * 64],
                qT[bp:bp + 64, ch * T + t0:ch * T + t0 + 512],
                start=True, stop=True),
                after=(P_toks if (tt == 0 and h == 0) else []) +
                      [p.psum_last_read[h % 4], tq], inc="am")
            t_ex = p.op("scalar", lambda e, ps=ps: e.activation(
                EL[:, :], ps[0:64, 0:512], AF.Exp), after=[t_mm], inc="ae")
            p.psum_last_read[h % 4] = t_ex
            t_rs = p.op("tensor", lambda e, ps=ps: e.matmul(
                ps[64:65, 0:512] if False else ps[0:1, 0:512],
                p.ones_col_bf[0:64, 0:1], EL[:, :], start=True, stop=True),
                after=[t_ex], inc="am")
            t_rc = p.op("vector", lambda e, ps=ps: e.reciprocal(
                row1[0:1, 0:512], ps[0:1, 0:512]), after=[t_rs], inc="av")
            p.psum_last_read[h % 4] = t_rc
            t_bc = p.op("tensor", lambda e, ps=ps: e.matmul(
                ps[0:64, 0:512], p.ones_row128[0:1, 0:64], row1[0:1, 0:512],
                start=True, stop=True), after=[t_rc], inc="am")
            t_rb = p.op("vector", lambda e, ps=ps: e.tensor_copy(
                rb[:, :], ps[0:64, 0:512]), after=[t_bc], inc="av")
            p.psum_last_read[h % 4] = t_rb
            t_y = p.op("tensor", lambda e, sl=sl, ps_y=ps_y: e.matmul(
                ps_y[0:64, 0:512], Z[:, sl], EL[:, :], start=True, stop=True),
                after=[t_ex, p.psum_last_read[4 + h % 2]], inc="am")
            t_ev = p.op("vector", lambda e, h=h, bp=bp, ch=ch, ps_y=ps_y:
                        e.tensor_mul(yT[bp:bp + 64, ch * 512:(ch + 1) * 512],
                                     ps_y[0:64, 0:512], rb[:, :]),
                        after=[t_y, t_rb], inc="av")
            p.psum_last_read[4 + h % 2] = t_ev
            y_toks.append(t_ev)
        ev_toks = []
        for oc in range(8):
            bank = 6 + oc % 2
            ps = p.psum_banks[bank]
            mm = None
            for kc in range(8):
                mm = p.op("tensor", lambda e, kc=kc, oc=oc, ps=ps: e.matmul(
                    ps[:, 0:512], wp[:, kc * 1024 + oc * 128:
                                     kc * 1024 + oc * 128 + 128],
                    yT[:, kc * 512:(kc + 1) * 512],
                    start=(kc == 0), stop=(kc == 7)),
                    after=(y_toks + [twp, p.psum_last_read[bank]])
                    if kc == 0 else [], inc="am" if kc == 7 else None)
            ev = p.op("vector", lambda e, oc=oc, ps=ps: e.tensor_add(
                xa_stage[:, oc * 512:(oc + 1) * 512], ps[:, 0:512],
                xt[:, oc * 512:(oc + 1) * 512]),
                after=[mm, txt, xa_free], inc="av")
            p.psum_last_read[bank] = ev
            ev_toks.append(ev)
        st = p.dma("sync", xa_o.ap().rearrange("(c p) t -> p c t", p=128)
                   [:, :, t0:t0 + 512], fm_sb(xa_stage, 8, 512),
                   after=ev_toks, inc="st")
        xa_free = st
        xt_free = st
        st_toks.append(st)

    stp = p.dma("sync", P_o.ap(), Pdbg_sb[:, :], after=[("pv", p.sem_plan["pv"])],
                inc="st")
    st_toks.append(stp)
    p.barrier_wait("sync", st_toks)
    return p.build()



def emit_ln2(p, src_fn, nchunks, tw, sq, tok_in, name, eps=1e-5,
             out_f32_fn=None, out_bf_fn=None, gamma_col=None, beta_col=None,
             scratch=None):
    D_full = nchunks * 128
    if scratch is None:
        stats = p.sbuf(name + "_st", [1, 3 * 512], F32)
        tmp = p.sbuf(name + "_tp", [128, 512], F32)
    else:
        stats, tmp = scratch
    assert tw <= 512
    ps_s, ps_q, ps_b = p.psum_banks[5], p.psum_banks[6], p.psum_banks[7]
    sq_toks = []
    for c in range(nchunks):
        tq = p.op("scalar", lambda e, c=c: e.activation(
            sq[:, c * 512:c * 512 + tw], src_fn(c), AF.Square),
            after=[tok_in], inc="lnsq")
        sq_toks.append(tq)
    ones_in = (p.ones_col_bf if str(src_fn(0).dtype) == "dt.bfloat16"
               else p.ones_col128)
    mm_tok = None
    for c in range(nchunks):
        p.op("tensor", lambda e, c=c: e.matmul(
            ps_s[0:1, 0:tw], ones_in[:, 0:1], src_fn(c),
            start=(c == 0), stop=(c == nchunks - 1)),
            after=[tok_in, p.psum_last_read[5]] if c == 0 else ())
        mm_tok = p.op("tensor", lambda e, c=c: e.matmul(
            ps_q[0:1, 0:tw], p.ones_col128[:, 0:1], sq[:, c * 512:c * 512 + tw],
            start=(c == 0), stop=(c == nchunks - 1)),
            after=[sq_toks[c]] + ([p.psum_last_read[6]] if c == 0 else []),
            inc="lnmm" if c == nchunks - 1 else None)
    mu, mr, rstd = stats[0:1, 0:tw], stats[0:1, 512:512 + tw], stats[0:1, 1024:1024 + tw]
    w1 = p.op("vector", lambda e: e.tensor_scalar(
        mu, ps_s[0:1, 0:tw], 1.0 / D_full, None, ALU.mult), after=[mm_tok],
        inc="lnvd")
    w2 = p.op("vector", lambda e: e.tensor_scalar(
        rstd, ps_q[0:1, 0:tw], 1.0 / D_full, None, ALU.mult), after=[mm_tok],
        inc="lnvd")
    w3 = p.op("vector", lambda e: e.tensor_mul(mr, mu, mu), after=[w1],
              inc="lnvd")
    w4 = p.op("vector", lambda e: e.tensor_sub(rstd, rstd, mr), after=[w2, w3],
              inc="lnvd")
    w5 = p.op("vector", lambda e: e.tensor_scalar(rstd, rstd, eps, None, ALU.add),
              after=[w4], inc="lnvd")
    w6 = p.op("scalar", lambda e: e.activation(rstd, rstd, AF.Ln), after=[w5],
              inc="lnrd")
    rstd_done = p.op("scalar", lambda e: e.activation(rstd, rstd, AF.Exp,
                     scale=-0.5), after=[w6], inc="lnrd")
    mr_done = p.op("vector", lambda e: e.tensor_mul(mr, mu, rstd),
                   after=[rstd_done], inc="lnvd")
    bc_r = p.op("tensor", lambda e: e.matmul(
        ps_b[:, 0:tw], p.ones_row128[0:1, :], rstd, start=True, stop=True),
        after=[rstd_done, p.psum_last_read[7]])
    bc_m = p.op("tensor", lambda e: e.matmul(
        ps_s[:, 0:tw], p.ones_row128[0:1, :], mr, start=True, stop=True),
        after=[mr_done], inc="lnmm")
    last = None
    for c in range(nchunks):
        t1 = p.op("vector", lambda e, c=c: e.tensor_sub(
            tmp[:, 0:tw], src_fn(c), ps_s[:, 0:tw]), after=[bc_m, tok_in],
            inc="lnev")
        if gamma_col is not None:
            t2 = p.op("vector", lambda e, c=c: e.tensor_mul(
                tmp[:, 0:tw], tmp[:, 0:tw], ps_b[:, 0:tw]),
                after=[bc_r, t1], inc="lnev")
            dsts = []
            if out_f32_fn is not None:
                dsts.append(out_f32_fn(c))
            if out_bf_fn is not None:
                dsts.append(out_bf_fn(c))
            for dst in dsts:
                last = p.op("vector", lambda e, c=c, dst=dst: e.tensor_scalar(
                    dst, tmp[:, 0:tw], gamma_col[:, c:c + 1],
                    beta_col[:, c:c + 1], ALU.mult, ALU.add),
                    after=[t2], inc="lnev")
        else:
            outs = []
            if out_f32_fn is not None:
                outs.append(out_f32_fn(c))
            if out_bf_fn is not None:
                outs.append(out_bf_fn(c))
            for dst in outs:
                last = p.op("vector", lambda e, c=c, dst=dst: e.tensor_mul(
                    dst, tmp[:, 0:tw], ps_b[:, 0:tw]), after=[bc_r, t1],
                    inc="lnev")
    p.psum_last_read[5] = last
    p.psum_last_read[6] = w4
    p.psum_last_read[7] = last
    return last



HALO = 1152
TEXT = HALO + 2048
NQT = 6
TQK = 3072


def build_ffn_attn(do_ln):
    p = Prog("ffnA")
    s_i = p.dram("s_ext", [D, TEXT], F32, kind="ExternalInput")
    wqkv_i = p.dram("wqkv", [D, 1536], F32, kind="ExternalInput")
    wo_i = p.dram("wo", [512, D], F32, kind="ExternalInput")
    qs_i = p.dram("qs", [128, 2], F32, kind="ExternalInput")
    id_i = p.dram("ident128", [128, 128], F32, kind="ExternalInput")
    tabs_i = p.dram("rtabs", [128, 4 * TEXT], BF16, kind="ExternalInput")
    mask_i = p.dram("amask", [128, 17 * 640], BF16, kind="ExternalInput")
    xmid_o = p.dram("xmid", [D, 2176], F32, kind="ExternalOutput")
    x0_scr = p.dram("x0scr", [D, 2176], F32)

    p.psum_init()
    init_toks = p.init_ones()
    qrot = p.sbuf("qrot", [128, 4 * TQK], BF16)
    krot = p.sbuf("krot", [128, 4 * TQK], BF16)
    vtok = p.sbuf("vtokf", [128, 24 * 512], BF16)
    qscol = p.sbuf("qscol", [128, 4], F32)
    srow = p.sbuf("srow", [1, 512], F32)
    eye_bf = p.sbuf("eye_bf", [128, 128], BF16)
    nmc = p.sbuf("nmc", [128, 4], F32)
    ln_stats = p.sbuf("ln_stats", [1, 3 * 512], F32)
    ln_tmp = p.sbuf("ln_tmp", [128, 512], F32)
    cv = Carve(p, 30900)
    xa_t = cv.f32(8 * 516)
    xln_t = cv.f32(8 * 520)
    sqs = cv.f32(8 * 512)
    sqn = sqs[:, 0:4 * 512]
    rb = cv.f32(2 * 512)
    xhat_t = cv.bf16(8 * 520)
    wqkv = cv.bf16(8 * 1536)
    wrot = cv.bf16(8 * 1024)
    qraw = cv.bf16(4 * 512)
    qrr = cv.bf16(4 * 512)
    kraw = cv.bf16(4 * 512)
    krr = cv.bf16(4 * 512)
    tabs = cv.bf16(4 * 512)
    cv.reset()
    masks = cv.bf16(17 * 640)
    x0_t = cv.f32(8 * 512)
    xm_st = cv.f32(8 * 512)
    obuf = cv.bf16(4 * 512)
    EA = cv.bf16(512)
    EB = cv.bf16(128)
    aT = cv.bf16(5 * 128)
    wo = cv.bf16(4 * 1024)

    tqs = p.dma("sync", qscol[:, 0:2], qs_i.ap(), inc="ld",
                allow_slow_non_contiguous=True)
    tide = p.dma("gpsimd", eye_bf[:, :], id_i.ap(), inc="ldw")
    qsr_toks = []
    for b0 in (0, 64):
        qsr_toks.append(p.op("vector", lambda e, b0=b0: e.tensor_copy(
            qscol[b0:b0 + 32, 2:4], qscol[b0 + 32:b0 + 64, 0:2]),
            after=[tqs], inc="init"))
        qsr_toks.append(p.op("vector", lambda e, b0=b0: e.tensor_copy(
            qscol[b0 + 32:b0 + 64, 2:4], qscol[b0:b0 + 32, 0:2]),
            after=[tqs], inc="init"))
    twq = p.dma("gpsimd", wqkv.rearrange("p (c x) -> p c x", c=8), fm_ap(wqkv_i), inc="ldw")
    wr_toks = []
    for rc in range(8):
        for g in range(2):
            src0 = wqkv.rearrange("p (c x) -> p c x", c=8)[
                :, rc, g * 512:(g + 1) * 512].rearrange("p (h d) -> p h d", h=8)
            dst0 = wrot.rearrange("p (c x) -> p c x", c=8)[
                :, rc, g * 512:(g + 1) * 512].rearrange("p (h d) -> p h d", h=8)
            wr_toks.append(p.op("vector", lambda e, src0=src0, dst0=dst0:
                e.tensor_scalar(dst0[:, :, 0:32], src0[:, :, 32:64], -1.0,
                                None, ALU.mult), after=[twq], inc="init"))
            wr_toks.append(p.op("vector", lambda e, src0=src0, dst0=dst0:
                e.tensor_copy(dst0[:, :, 32:64], src0[:, :, 0:32]),
                after=[twq], inc="init"))
    mz = p.op("vector", lambda e: e.memset(
        xhat_t.rearrange("p (c x) -> p c x", c=8)[:, :, 0:4], 0.0),
        inc="init")
    mz2 = p.op("vector", lambda e: e.memset(
        xln_t.rearrange("p (c x) -> p c x", c=8)[:, :, 0:4], 0.0),
        inc="init")
    r0tok = p.op("tensor", lambda e: None,
                 after=init_toks + qsr_toks + wr_toks + [mz, mz2, tide],
                 inc="ph0")
    ready0 = [r0tok]

    xa_free = None
    tab_free = None
    x0st_toks = []
    rot_prev = {"q": [], "k": []}
    carry_prev = []
    srow_free = [None]
    last_rot_tok = None
    for tt in range(NQT):
        toff = 128 + tt * 512
        ld = p.dma("sync", xa_t.rearrange("p (c t) -> p c t", c=8),
                   fm_ap(s_i)[:, :, toff - 4: toff + 512],
                   after=[xa_free], inc="ld")
        ltab = p.dma("sync", tabs.rearrange("p (g t) -> p g t", g=4),
                     tabs_i.ap().rearrange("p (g t) -> p g t", g=4)
                     [:, :, toff:toff + 512], after=[tab_free], inc="ld")
        carry_toks = []
        for c in range(8):
            if tt > 0:
                carry_toks.append(p.op("vector", lambda e, c=c: e.tensor_copy(
                    xln_t[:, c * 520:c * 520 + 4],
                    xln_t[:, c * 520 + 512:c * 520 + 516]),
                    after=carry_prev, inc="cp"))
                carry_toks.append(p.op("vector", lambda e, c=c: e.tensor_copy(
                    xhat_t[:, c * 520:c * 520 + 4],
                    xhat_t[:, c * 520 + 512:c * 520 + 516]),
                    after=carry_prev, inc="cp"))
        if do_ln:
            lt = emit_ln2(p, lambda c: xa_t[:, c * 516 + 4:c * 516 + 516],
                          8, 512, sqs, ld, name=f"ln{tt}",
                          out_f32_fn=lambda c: xln_t[:, c * 520 + 4:c * 520 + 516],
                          scratch=(ln_stats, ln_tmp))
        else:
            lt = None
            for c in range(8):
                lt = p.op("vector", lambda e, c=c: e.tensor_copy(
                    xln_t[:, c * 520 + 4:c * 520 + 516],
                    xa_t[:, c * 516 + 4:c * 516 + 516]), after=[ld], inc="cp")
        cast_toks = list(carry_toks)
        for c in range(8):
            cast_toks.append(p.op("vector", lambda e, c=c: e.tensor_copy(
                xhat_t[:, c * 520 + 4:c * 520 + 516],
                xln_t[:, c * 520 + 4:c * 520 + 516]), after=[lt], inc="cp"))
        carry_prev = cast_toks
        lo = max(toff, 1024)
        w0 = toff + 512 - lo
        if w0 > 0:
            dst0 = lo - 1024
            s0 = 4 + (lo - toff)
            x0st_toks.append(p.dma("sync",
                fm_ap(x0_scr)[:, :, dst0:dst0 + w0],
                xln_t.rearrange("p (c t) -> p c t", c=8)[:, :, s0:s0 + w0],
                after=[lt], inc="st0"))
        ph_cast = p.op("tensor", lambda e: None, after=cast_toks, inc="ph1")

        def rhs_x(kc):
            off = 4 if kc < 4 else 3
            return xhat_t[:, kc * 520 + off: kc * 520 + off + 512]

        raw_toks = {}
        for (nm, wsb, col0, dst) in (("q", wqkv, 0, qraw), ("qr", wrot, 0, qrr),
                                     ("k", wqkv, 512, kraw),
                                     ("kr", wrot, 512, krr)):
            ev_l = []
            for oc in range(4):
                bank = oc % 2
                ps = p.psum_banks[bank]
                mm = None
                for kc in range(8):
                    mm = p.op("tensor", lambda e, kc=kc, oc=oc, ps=ps, wsb=wsb,
                              col0=col0: e.matmul(
                        ps[:, 0:512],
                        wsb.rearrange("p (c x) -> p c x", c=8)
                        [:, kc, col0 + oc * 128: col0 + oc * 128 + 128],
                        rhs_x(kc), start=(kc == 0), stop=(kc == 7)),
                        after=([ph_cast] + ready0 + [p.psum_last_read[bank]])
                        if kc == 0 else (), inc="mm" if kc == 7 else None)
                ev = p.op("scalar", lambda e, oc=oc, ps=ps, dst=dst: e.copy(
                    dst[:, oc * 512:(oc + 1) * 512], ps[:, 0:512]),
                    after=[mm] + rot_prev["q" if nm in ("q", "qr") else "k"],
                    inc="ev")
                p.psum_last_read[bank] = ev
                ev_l.append(ev)
            raw_toks[nm] = ev_l
        for sub in range(4):
            bank = 2 + sub % 2
            ps = p.psum_banks[bank]
            mm = None
            for kc in range(8):
                off = 4 if kc < 4 else 3
                lhs = xhat_t[:, kc * 520 + off + sub * 128:
                             kc * 520 + off + sub * 128 + 128]
                mm = p.op("tensor", lambda e, lhs=lhs, kc=kc, ps=ps: e.matmul(
                    ps[:, 0:512], lhs,
                    wqkv.rearrange("p (c x) -> p c x", c=8)[:, kc, 1024:1536],
                    start=(kc == 0), stop=(kc == 7)),
                    after=([ph_cast] + ready0 + [p.psum_last_read[bank]])
                    if kc == 0 else (), inc="mm" if kc == 7 else None)
            vchunk = 4 * tt + sub
            ev = p.op("scalar", lambda e, ps=ps, vchunk=vchunk: e.copy(
                vtok[:, vchunk * 512:(vchunk + 1) * 512], ps[:, 0:512]),
                after=[mm], inc="ev")
            p.psum_last_read[bank] = ev
        sqn_writers = []
        for (nm, rawb, rotb, dstb, tabcs, tabss, qsc, qsrc) in (
                ("q", qraw, qrr, qrot, 0, 1, 0, 2),
                ("k", kraw, krr, krot, 2, 3, 1, 3)):
            ev_l = raw_toks[nm] + raw_toks[nm + "r"]
            sq_t = []
            for oc in range(4):
                sq_t.append(p.op("scalar", lambda e, oc=oc, rawb=rawb:
                    e.activation(sqn[:, oc * 512:(oc + 1) * 512],
                                 rawb[:, oc * 512:(oc + 1) * 512], AF.Square),
                    after=ev_l + rot_prev[nm] + sqn_writers, inc="sq"))
            rot_done = []
            for pr in range(4):
                ssq_toks = []
                for h in (2 * pr, 2 * pr + 1):
                    ch, bp = h // 2, (h % 2) * 64
                    bank = 4 + h % 2
                    ps = p.psum_banks[bank]
                    ssq_toks.append(p.op("tensor", lambda e, ch=ch, bp=bp,
                        ps=ps: e.matmul(
                        ps[0:1, 0:512], p.ones_col128[bp:bp + 64, 0:1],
                        sqn[bp:bp + 64, ch * 512:(ch + 1) * 512],
                        start=True, stop=True),
                        after=[sq_t[ch], p.psum_last_read[bank]], inc="mm"))
                    p.psum_last_read[bank] = ssq_toks[-1]
                ph_ssq = p.op("vector", lambda e: None, after=ssq_toks,
                              inc="phs")
                for hh in range(2):
                    h = 2 * pr + hh
                    ch, bp = h // 2, (h % 2) * 64
                    bank = 4 + h % 2
                    ps = p.psum_banks[bank]
                    t2 = p.op("scalar", lambda e, ps=ps: e.activation(
                        srow[0:1, :], ps[0:1, 0:512], AF.Sqrt),
                        after=[ssq_toks[hh], srow_free[0]], inc="sq")
                    t3 = p.op("vector", lambda e: e.tensor_scalar(
                        srow[0:1, :], srow[0:1, :], 1e-12, None, ALU.max),
                        after=[t2], inc="vv")
                    t4 = p.op("vector", lambda e: e.reciprocal(
                        srow[0:1, :], srow[0:1, :]), after=[t3], inc="vv")
                    t5 = p.op("tensor", lambda e, ps=ps: e.matmul(
                        ps[0:64, 0:512], p.ones_row128[0:1, 0:64], srow[0:1, :],
                        start=True, stop=True), after=[t4], inc="mm")
                    srow_free[0] = t5
                    t6 = p.op("vector", lambda e, ps=ps, tabcs=tabcs, bp=bp:
                        e.tensor_mul(rb[bp:bp + 64, 0:512], ps[0:64, 0:512],
                                     tabs[bp:bp + 64,
                                          tabcs * 512:(tabcs + 1) * 512]),
                        after=[t5, ltab], inc="vv")
                    t7 = p.op("vector", lambda e, ps=ps, tabss=tabss, bp=bp:
                        e.tensor_mul(rb[bp:bp + 64, 512:1024], ps[0:64, 0:512],
                                     tabs[bp:bp + 64,
                                          tabss * 512:(tabss + 1) * 512]),
                        after=[t5, ltab, t6], inc="vv")
                    p.psum_last_read[bank] = t7
                    t8 = p.op("vector", lambda e, rawb=rawb, ch=ch, bp=bp,
                              qsc=qsc: e.tensor_scalar(
                        sqn[bp:bp + 64, ch * 512:(ch + 1) * 512],
                        rawb[bp:bp + 64, ch * 512:(ch + 1) * 512],
                        qscol[bp:bp + 64, qsc:qsc + 1], None,
                        ALU.mult), after=[t6, ph_ssq], inc="vv")
                    t9 = p.op("vector", lambda e, ch=ch, bp=bp: e.tensor_mul(
                        sqn[bp:bp + 64, ch * 512:(ch + 1) * 512],
                        sqn[bp:bp + 64, ch * 512:(ch + 1) * 512],
                        rb[bp:bp + 64, 0:512]), after=[t8], inc="vv")
                    t10 = p.op("vector", lambda e, rotb=rotb, ch=ch, bp=bp,
                               qsrc=qsrc: e.tensor_scalar(
                        rb[bp:bp + 64, 0:512],
                        rotb[bp:bp + 64, ch * 512:(ch + 1) * 512],
                        qscol[bp:bp + 64, qsrc:qsrc + 1], None, ALU.mult),
                        after=[t9], inc="vv")
                    t11 = p.op("vector", lambda e, bp=bp: e.tensor_mul(
                        rb[bp:bp + 64, 0:512], rb[bp:bp + 64, 0:512],
                        rb[bp:bp + 64, 512:1024]), after=[t10, t7], inc="vv")
                    dcol = 512 * tt
                    t12 = p.op("vector", lambda e, dstb=dstb, dcol=dcol, ch=ch,
                               bp=bp: e.tensor_add(
                        dstb[bp:bp + 64, ch * TQK + dcol:
                             ch * TQK + dcol + 512],
                        sqn[bp:bp + 64, ch * 512:(ch + 1) * 512],
                        rb[bp:bp + 64, 0:512]), after=[t11, t9], inc="rot")
                    rot_done.append(t12)
            rot_prev[nm] = rot_done
            sqn_writers = rot_done
            last_rot_tok = rot_done[-1]
        xa_free = last_rot_tok if tt >= 0 else lt
        if xa_free is None:
            xa_free = lt
        tab_free = last_rot_tok

    tmask = p.dma("sync", masks[:, :], mask_i.ap(),
                  after=rot_prev["q"] + rot_prev["k"] + x0st_toks, inc="ld")
    two = p.dma("gpsimd", wo.rearrange("p (c x) -> p c x", c=4), fm_ap(wo_i),
                after=rot_prev["q"] + rot_prev["k"] + x0st_toks, inc="ldw")
    ph2 = p.op("tensor", lambda e: None,
               after=rot_prev["q"] + rot_prev["k"] + [tmask], inc="ph2")
    ph2v = p.op("vector", lambda e: None,
                after=rot_prev["q"] + rot_prev["k"], inc="ph2v")
    ph2s = p.op("scalar", lambda e: None,
                after=rot_prev["q"] + rot_prev["k"], inc="ph2s")

    st_toks = []
    x0_free = None
    xm_free = None
    e_free = [ph2s, ph2s]
    at_free = ph2v
    prev_x2 = ph2s
    for grp in range(5):
        ci_list = [-1] if grp == 0 else [4 * (grp - 1) + k for k in range(4)]
        gcol0 = (ci_list[0] + 1) * 128
        gw = 128 * len(ci_list)
        lx0 = p.dma("sync", x0_t.rearrange("p (c t) -> p c t", c=8)[:, :, 0:gw],
                    fm_ap(x0_scr)[:, :, gcol0:gcol0 + gw],
                    after=[x0_free] + x0st_toks, inc="ld")
        ob_toks = []
        for ci in ci_list:
            qcol = (ci + 8) * 128
            icol = (ci + 1) * 128 - gcol0
            for h in range(8):
                ch, bp = h // 2, (h % 2) * 64
                psA, psB = p.psum_banks[0], p.psum_banks[1]
                psO = p.psum_banks[3]
                jcol = (ci + 4) * 128
                mmA = p.op("tensor", lambda e, ch=ch, bp=bp, qcol=qcol,
                           jcol=jcol, psA=psA: e.matmul(
                    psA[:, 0:512],
                    qrot[bp:bp + 64, ch * TQK + qcol: ch * TQK + qcol + 128],
                    krot[bp:bp + 64, ch * TQK + jcol: ch * TQK + jcol + 512],
                    start=True, stop=True),
                    after=[ph2, p.psum_last_read[0]], inc="am")
                mmB = p.op("tensor", lambda e, ch=ch, bp=bp, qcol=qcol,
                           psB=psB: e.matmul(
                    psB[:, 0:128],
                    qrot[bp:bp + 64, ch * TQK + qcol: ch * TQK + qcol + 128],
                    krot[bp:bp + 64, ch * TQK + qcol: ch * TQK + qcol + 128],
                    start=True, stop=True),
                    after=[ph2, p.psum_last_read[1]], inc="am")
                mcol = (ci + 1) * 640
                tmA = p.op("vector", lambda e, psA=psA, mcol=mcol: e.tensor_add(
                    psA[:, 0:512], psA[:, 0:512], masks[:, mcol:mcol + 512]),
                    after=[mmA, ph2v], inc="av")
                tmB = p.op("vector", lambda e, psB=psB, mcol=mcol: e.tensor_add(
                    psB[:, 0:128], psB[:, 0:128],
                    masks[:, mcol + 512:mcol + 640]), after=[mmB, tmA], inc="av")
                n1 = p.op("vector", lambda e, psA=psA: e.reduce_max(
                    nmc[:, 0:1], psA[:, 0:512], axis=AX.X, negate=True),
                    after=[tmA, prev_x2], inc="av")
                n2 = p.op("vector", lambda e, psB=psB: e.reduce_max(
                    nmc[:, 1:2], psB[:, 0:128], axis=AX.X, negate=True),
                    after=[tmB, n1], inc="av")
                n3 = p.op("vector", lambda e: e.tensor_tensor(
                    nmc[:, 2:3], nmc[:, 0:1], nmc[:, 1:2], ALU.min),
                    after=[n2], inc="av")
                n4 = p.op("vector", lambda e: e.tensor_scalar(
                    nmc[:, 2:3], nmc[:, 2:3], 8.0, 1e30, ALU.mult, ALU.min),
                    after=[n3], inc="av")
                x1 = p.op("scalar", lambda e, psA=psA: e.activation(
                    EA[:, :], psA[:, 0:512], AF.Exp, bias=nmc[:, 2:3],
                    scale=8.0), after=[n4, e_free[0]], inc="ae")
                x2 = p.op("scalar", lambda e, psB=psB: e.activation(
                    EB[:, :], psB[:, 0:128], AF.Exp, bias=nmc[:, 2:3],
                    scale=8.0), after=[n4, e_free[1], x1], inc="ae")
                p.psum_last_read[0] = x1
                p.psum_last_read[1] = x2
                prev_x2 = x2
                s1 = p.op("vector", lambda e: e.reduce_sum(
                    nmc[:, 0:1], EA[:, :], axis=AX.X), after=[x1], inc="av")
                s2 = p.op("vector", lambda e: e.reduce_sum(
                    nmc[:, 1:2], EB[:, :], axis=AX.X), after=[x2, s1], inc="av")
                s3 = p.op("vector", lambda e: e.tensor_add(
                    nmc[:, 2:3], nmc[:, 0:1], nmc[:, 1:2]), after=[s2], inc="av")
                s4 = p.op("vector", lambda e: e.tensor_scalar(
                    nmc[:, 2:3], nmc[:, 2:3], 1e-30, None, ALU.max),
                    after=[s3], inc="av")
                s5 = p.op("vector", lambda e: e.reciprocal(
                    nmc[:, 3:4], nmc[:, 2:3]), after=[s4], inc="av")
                a1 = p.op("vector", lambda e: e.tensor_scalar(
                    EA[:, :], EA[:, :], nmc[:, 3:4], None, ALU.mult),
                    after=[s5], inc="av")
                a2 = p.op("vector", lambda e: e.tensor_scalar(
                    EB[:, :], EB[:, :], nmc[:, 3:4], None, ALU.mult),
                    after=[s5, a1], inc="av")
                tr_toks = []
                for jj in range(5):
                    srcap = (EA[:, jj * 128:(jj + 1) * 128] if jj < 4
                             else EB[:, 0:128])
                    tr = p.op("tensor", lambda e, srcap=srcap: e.transpose(
                        p.pbf(2)[0:128, 0:128], srcap, eye_bf[:, :]),
                        after=[a1 if jj < 4 else a2, p.psum_last_read[2]],
                        inc="am")
                    tv = p.op("vector", lambda e, jj=jj: e.tensor_copy(
                        aT[:, jj * 128:(jj + 1) * 128], p.pbf(2)[0:128, 0:128]),
                        after=[tr, at_free], inc="av")
                    p.psum_last_read[2] = tv
                    tr_toks.append(tv)
                mmo = None
                for jj in range(5):
                    vchunk = (ci + 4) + jj
                    mmo = p.op("tensor", lambda e, vchunk=vchunk, jj=jj, h=h,
                               psO=psO: e.matmul(
                        psO[0:64, 0:128],
                        vtok[:, vchunk * 512 + h * 64:
                             vchunk * 512 + h * 64 + 64],
                        aT[:, jj * 128:(jj + 1) * 128],
                        start=(jj == 0), stop=(jj == 4)),
                        after=[tr_toks[jj]] +
                              ([p.psum_last_read[3]] if jj == 0 else []),
                        inc="am" if jj == 4 else None)
                evo = p.op("vector", lambda e, h=h, icol=icol, psO=psO:
                    e.tensor_copy(
                        obuf[(h % 2) * 64:(h % 2) * 64 + 64,
                             (h // 2) * 512 + icol:
                             (h // 2) * 512 + icol + 128],
                        psO[0:64, 0:128]), after=[mmo], inc="av")
                p.psum_last_read[3] = evo
                at_free = evo
                e_free = [tr_toks[3], tr_toks[4]]
                ob_toks.append(evo)
        ev_g = []
        for oc in range(8):
            bank = 6 + oc % 2
            ps = p.psum_banks[bank]
            mm = None
            for kc in range(4):
                mm = p.op("tensor", lambda e, kc=kc, oc=oc, ps=ps, gw=gw:
                    e.matmul(
                    ps[:, 0:gw],
                    wo.rearrange("p (c x) -> p c x", c=4)
                    [:, kc, oc * 128:(oc + 1) * 128],
                    obuf[:, kc * 512: kc * 512 + gw],
                    start=(kc == 0), stop=(kc == 3)),
                    after=(ob_toks + [p.psum_last_read[bank], two])
                    if kc == 0 else (), inc="am" if kc == 3 else None)
            ev = p.op("vector", lambda e, oc=oc, ps=ps, gw=gw: e.tensor_add(
                xm_st[:, oc * 512: oc * 512 + gw], ps[:, 0:gw],
                x0_t[:, oc * 512: oc * 512 + gw]),
                after=[mm, lx0, xm_free], inc="av")
            p.psum_last_read[bank] = ev
            ev_g.append(ev)
        st = p.dma("sync", fm_ap(xmid_o)[:, :, gcol0:gcol0 + gw],
                   xm_st.rearrange("p (c t) -> p c t", c=8)[:, :, 0:gw], after=ev_g, inc="st")
        x0_free = st
        xm_free = st
        st_toks.append(st)
    qd_o = p.dram("q_dbg", [512, TQK], BF16, kind="ExternalOutput")
    kd_o = p.dram("k_dbg", [512, TQK], BF16, kind="ExternalOutput")
    sd1 = p.dma("sync", qd_o.ap().rearrange("(c p) t -> p c t", p=128),
                qrot.ap().rearrange("p (c t) -> p c t", c=4), after=[ph2],
                inc="st")
    sd2 = p.dma("sync", kd_o.ap().rearrange("(c p) t -> p c t", p=128),
                krot.ap().rearrange("p (c t) -> p c t", c=4), after=[ph2],
                inc="st")
    p.barrier_wait("sync", st_toks + [sd1, sd2])
    return p.build()



def build_ffn_ffn(add_extra_resid):
    p = Prog("ffnB")
    xm_i = p.dram("xmid_ext", [D, 2049], F32, kind="ExternalInput")
    w1_i = p.dram("w1", [D, 5460], F32, kind="ExternalInput")
    b1a_i = p.dram("b1a", [22, 128], F32, kind="ExternalInput")
    b1g_i = p.dram("b1g", [22, 128], F32, kind="ExternalInput")
    w2_i = p.dram("w2", [2730, D], F32, kind="ExternalInput")
    b2_i = p.dram("b2", [8, 128], F32, kind="ExternalInput")
    lng_i = p.dram("lng", [8, 128], F32, kind="ExternalInput")
    lnb_i = p.dram("lnb", [8, 128], F32, kind="ExternalInput")
    out_o = p.dram("s_out", [D, T], F32, kind="ExternalOutput")
    if add_extra_resid:
        xr_i = p.dram("x_resid", [D, T], F32, kind="ExternalInput")

    p.psum_init()
    init_toks = p.init_ones()
    NH = 22
    xm_t = p.sbuf("xm_t", [128, 8 * 513], F32)
    x2h = p.sbuf("x2h", [128, 8 * 512], BF16)
    sqs = p.sbuf("sqs", [128, 8 * 512], F32)
    w1b = p.sbuf("w1b", [128, 2 * 8 * 512], BF16)
    w2b = p.sbuf("w2b", [128, 2 * NH * 128], BF16)
    inter = p.sbuf("inter", [128, NH * 512], BF16)
    glb = p.sbuf("glb", [128, 512], BF16)
    asb = p.sbuf("asb", [128, 512], BF16)
    lngc = p.sbuf("lngc", [128, 8], F32)
    lnbc = p.sbuf("lnbc", [128, 8], F32)
    b1ac = p.sbuf("b1ac", [128, 22], F32)
    b1gc = p.sbuf("b1gc", [128, 22], F32)
    b2c = p.sbuf("b2c", [128, 8], F32)
    out_st = p.sbuf("out_st", [128, 8 * 512], F32)
    xr_t = p.sbuf("xr_t", [128, 8 * 512], F32)
    ln_stats = p.sbuf("ln_stats", [1, 3 * 512], F32)
    ln_tmp = p.sbuf("ln_tmp", [128, 512], F32)

    tg = p.dma("sync", lngc[:, :], lng_i.ap().rearrange("c p -> p c"),
               inc="ld", allow_slow_non_contiguous=True)
    tb = p.dma("sync", lnbc[:, :], lnb_i.ap().rearrange("c p -> p c"),
               inc="ld", allow_slow_non_contiguous=True)
    t1a = p.dma("sync", b1ac[:, :], b1a_i.ap().rearrange("c p -> p c"),
                inc="ld", allow_slow_non_contiguous=True)
    t1g = p.dma("sync", b1gc[:, :], b1g_i.ap().rearrange("c p -> p c"),
                inc="ld", allow_slow_non_contiguous=True)
    t2b = p.dma("sync", b2c[:, :], b2_i.ap().rearrange("c p -> p c"),
                inc="ld", allow_slow_non_contiguous=True)
    r0 = p.op("tensor", lambda e: None,
              after=init_toks + [tg, tb, t1a, t1g, t2b], inc="ph0")
    r0v = p.op("vector", lambda e: None,
               after=init_toks + [tg, tb, t1a, t1g, t2b], inc="ph0v")
    r0s = p.op("scalar", lambda e: None,
               after=init_toks + [tg, tb, t1a, t1g, t2b], inc="ph0s")

    xm_free, xr_free = None, None
    w1_free = [None, None]
    w2_free = [None, None]
    st_toks = []
    x2_free = None
    int_free = [None] * NH
    gl_free, as_free = r0s, r0s
    for tt in range(4):
        t0 = tt * 512
        ld = p.dma("sync", fm_sb(xm_t, 8, 513),
                   fm_ap(xm_i)[:, :, t0:t0 + 513], after=[xm_free], inc="ld")
        if add_extra_resid:
            lxr = p.dma("sync", fm_sb(xr_t, 8, 512),
                        fm_ap(xr_i)[:, :, t0:t0 + 512], after=[xr_free],
                        inc="ld")

        def src_fn(c):
            off = 1 if c < 4 else 0
            return xm_t[:, c * 513 + off: c * 513 + off + 512]
        lt = emit_ln2(p, src_fn, 8, 512, sqs, ld, name=f"l2_{tt}",
                      out_bf_fn=lambda c: x2h[:, c * 512:(c + 1) * 512],
                      gamma_col=lngc, beta_col=lnbc,
                      scratch=(ln_stats, ln_tmp))
        ltb = p.op("tensor", lambda e: None, after=[lt, x2_free, r0], inc="phx")
        int_toks = []
        for hcg in range(6):
            pass
        for j in range(NH):
            cw = 128 if j < 21 else 42
            slot = j % 2
            lwa = p.dma("gpsimd",
                        w1b.ap().rearrange("p (s c x) -> p s c x", s=2, c=8)
                        [:, slot, :, 0:cw],
                        fm_ap(w1_i)[:, :, j * 128:j * 128 + cw],
                        after=[w1_free[slot]], inc="ldw")
            lwg = p.dma("gpsimd",
                        w1b.ap().rearrange("p (s c x) -> p s c x", s=2, c=8)
                        [:, slot, :, 256:256 + cw],
                        fm_ap(w1_i)[:, :, 2730 + j * 128:2730 + j * 128 + cw],
                        after=[w1_free[slot]], inc="ldw")
            psg = p.psum_banks[0 + slot]
            psa = p.psum_banks[2 + slot]
            mmg, mma = None, None
            for kc in range(8):
                mmg = p.op("tensor", lambda e, kc=kc, slot=slot, cw=cw,
                           psg=psg: e.matmul(
                    psg[0:cw, 0:512],
                    w1b.ap().rearrange("p (s c x) -> p s c x", s=2, c=8)
                    [:, slot, kc, 256:256 + cw],
                    x2h.ap().rearrange("p (c x) -> p c x", c=8)[:, kc, :],
                    start=(kc == 0), stop=(kc == 7)),
                    after=([ltb, lwg, p.psum_last_read[0 + slot]])
                    if kc == 0 else (), inc="mm" if kc == 7 else None)
            for kc in range(8):
                mma = p.op("tensor", lambda e, kc=kc, slot=slot, cw=cw,
                           psa=psa: e.matmul(
                    psa[0:cw, 0:512],
                    w1b.ap().rearrange("p (s c x) -> p s c x", s=2, c=8)
                    [:, slot, kc, 0:cw],
                    x2h.ap().rearrange("p (c x) -> p c x", c=8)[:, kc, :],
                    start=(kc == 0), stop=(kc == 7)),
                    after=([ltb, lwa, p.psum_last_read[2 + slot]])
                    if kc == 0 else (), inc="mm" if kc == 7 else None)
            ge = p.op("scalar", lambda e, j=j, cw=cw, psg=psg: e.activation(
                glb[0:cw, :], psg[0:cw, 0:512], AF.Gelu,
                bias=b1gc[0:cw, j:j + 1]), after=[mmg, gl_free], inc="ge")
            ae = p.op("scalar", lambda e, j=j, cw=cw, psa=psa: e.activation(
                asb[0:cw, :], psa[0:cw, 0:512], AF.Identity,
                bias=b1ac[0:cw, j:j + 1]), after=[mma, as_free, ge], inc="ge")
            p.psum_last_read[0 + slot] = ge
            p.psum_last_read[2 + slot] = ae
            w1_free[slot] = mma
            it = p.op("vector", lambda e, j=j, cw=cw: e.tensor_mul(
                inter[0:cw, j * 512:(j + 1) * 512], asb[0:cw, :], glb[0:cw, :]),
                after=[ae, int_free[j]], inc="iv")
            gl_free, as_free = it, it
            int_toks.append(it)
            int_free[j] = it
        ph_int = p.op("tensor", lambda e: None, after=int_toks, inc="phi")
        ev_g = []
        for oc in range(8):
            slot = oc % 2
            lw2 = p.dma("gpsimd",
                        w2b.ap().rearrange("p (s c x) -> p s c x", s=2, c=NH)
                        [:, slot, 0:21, :],
                        w2_i.ap()[0:2688, oc * 128:(oc + 1) * 128]
                        .rearrange("(c p) o -> p c o", p=128),
                        after=[w2_free[slot]], inc="ldw")
            lw2b = p.dma("gpsimd",
                         w2b.ap().rearrange("p (s c x) -> p s c x", s=2, c=NH)
                         [0:42, slot, 21, :],
                         w2_i.ap()[2688:2730, oc * 128:(oc + 1) * 128],
                         after=[w2_free[slot]], inc="ldw")
            bank = 4 + oc % 2
            ps = p.psum_banks[bank]
            mm = None
            for kc in range(NH):
                kw = 128 if kc < 21 else 42
                mm = p.op("tensor", lambda e, kc=kc, kw=kw, slot=slot, ps=ps:
                    e.matmul(
                    ps[:, 0:512],
                    w2b.ap().rearrange("p (s c x) -> p s c x", s=2, c=NH)
                    [0:kw, slot, kc, :],
                    inter[0:kw, kc * 512:(kc + 1) * 512],
                    start=(kc == 0), stop=(kc == NH - 1)),
                    after=([ph_int, lw2, lw2b, p.psum_last_read[bank]])
                    if kc == 0 else (), inc="mm" if kc == NH - 1 else None)
            w2_free[slot] = mm
            e1 = p.op("vector", lambda e, oc=oc, ps=ps: e.tensor_scalar(
                out_st[:, oc * 512:(oc + 1) * 512], ps[:, 0:512],
                b2c[:, oc:oc + 1], None, ALU.add), after=[mm, r0v, st_toks[-1]
                if st_toks else None], inc="ov")
            e2 = p.op("vector", lambda e, oc=oc, ps=ps: e.tensor_add(
                out_st[:, oc * 512:(oc + 1) * 512],
                out_st[:, oc * 512:(oc + 1) * 512],
                xm_t[:, oc * 513 + 1: oc * 513 + 513]), after=[e1], inc="ov")
            if add_extra_resid:
                e2 = p.op("vector", lambda e, oc=oc: e.tensor_add(
                    out_st[:, oc * 512:(oc + 1) * 512],
                    out_st[:, oc * 512:(oc + 1) * 512],
                    xr_t[:, oc * 512:(oc + 1) * 512]), after=[e2, lxr],
                    inc="ov")
            ev_g.append(e2)
        st = p.dma("sync", fm_ap(out_o)[:, :, t0:t0 + 512],
                   fm_sb(out_st, 8, 512), after=ev_g, inc="st")
        st_toks.append(st)
        xm_free = st
        xr_free = st
        x2_free = st
    p.barrier_wait("sync", st_toks)
    return p.build()



def build_mamba_local(glb=2, stage=99, nch=16, hn=16, hcut=0):
    p = Prog("mb")
    xb_i = p.dram("xb_ext", [D, 2176], F32, kind="ExternalInput")
    wT_i = p.dram("inprojT", [D, 4352], F32, kind="ExternalInput")
    wN_i = p.dram("inprojN", [4352, D], F32, kind="ExternalInput")
    cw_i = p.dram("convw", [128, 17 * 4], F32, kind="ExternalInput")
    cb_i = p.dram("convb", [128, 17], F32, kind="ExternalInput")
    dtb_i = p.dram("dtb", [16, 1], F32, kind="ExternalInput")
    al_i = p.dram("alog", [16, 1], F32, kind="ExternalInput")
    dd_i = p.dram("dval", [1, 16], F32, kind="ExternalInput")
    xnw_i = p.dram("xnw", [8, 128], F32, kind="ExternalInput")
    xnb_i = p.dram("xnb", [8, 128], F32, kind="ExternalInput")
    id_i = p.dram("ident128", [128, 128], F32, kind="ExternalInput")
    tri_i = p.dram("trimask", [128, 128], F32, kind="ExternalInput")
    y_o = p.dram("y_locT", [2048, T], BF16, kind="ExternalOutput")
    ch_o = p.dram("chT", [1024, T], BF16, kind="ExternalOutput")
    z_o = p.dram("zT", [2048, T], BF16, kind="ExternalOutput")
    sc_o = p.dram("S_core", [64, 2048], F32, kind="ExternalOutput")
    ac_o = p.dram("A_core", [16, 1], F32, kind="ExternalOutput")
    att_o = p.dram("att_dbg", [128, 3 * 128], BF16, kind="ExternalOutput")
    xc_o = p.dram("xc_dbg", [128, 2048], BF16, kind="ExternalOutput")
    xp_o = p.dram("xp_dbg", [128, 300], BF16, kind="ExternalOutput")
    rn_o = p.dram("rn_dbg", [128, 34], F32, kind="ExternalOutput")

    p.psum_init()
    init_toks = p.init_ones()
    xpre = p.sbuf("xpre", [128, 17 * 2052], BF16)
    dtrow = p.sbuf("dtrow", [16, 4 * T], F32)
    onesrow16 = p.sbuf("ones16", [16, T], F32)
    Dcols = p.sbuf("Dcols", [128, 16], F32)
    Srun = p.sbuf("Srun", [64, 16 * 128], F32)
    rncol = p.sbuf("rncol", [128, 34], F32)
    cwc = p.sbuf("cwc", [128, 17 * 4], F32)
    cbc = p.sbuf("cbc", [128, 17], F32)
    dcol16 = p.sbuf("dcol16", [16, 8], F32)
    xnwc = p.sbuf("xnwc", [128, 8], F32)
    xnbc = p.sbuf("xnbc", [128, 8], F32)
    eye_bf = p.sbuf("eyebf", [128, 128], BF16)
    eye_f = p.sbuf("eyef", [128, 128], F32)
    tri = p.sbuf("tri", [128, 128], F32)
    negrow = p.sbuf("negrow", [1, 128], F32)
    ln_stats = p.sbuf("ln_st", [1, 3 * 512], F32)
    ln_tmp = p.sbuf("ln_tp", [128, 512], F32)
    cdc = p.sbuf("cdc", [64, 16], F32)
    cumT = p.sbuf("cumT", [128, 32], F32)
    cumrow = p.sbuf("cumrow", [1, 16 * 128], F32)
    expcrow = p.sbuf("expcrow", [1, 16 * 128], F32)
    expg_bf = p.sbuf("expgbf", [1, 16 * 128], BF16)
    cbg = p.sbuf("cbg", [64, 128], F32)
    prow = p.sbuf("prow", [1, 16], F32)
    eprow = p.sbuf("eprow", [1, 16], F32)
    cv = Carve(p, 15200)
    xb_t = cv.f32(8 * 512)
    sqs = cv.f32(8 * 512)
    wn = cv.bf16(1024)
    u1_t = cv.bf16(8 * 512)
    u2_t = cv.bf16(8 * 512)
    zbuf = cv.bf16(512)
    wsb = cv.bf16(2 * 8 * 256)
    cv.reset()
    xcT = cv.f32(0)
    xcfc = cv.bf16(16 * 128)
    xcT = cv.bf16(16 * 128)
    ybuf = cv.bf16(16 * 128)
    cbuf = cv.bf16(8 * 128)
    Bb = cv.bf16(128)
    Cb = cv.bf16(128)
    Btok = cv.bf16(64)
    Bsc = cv.bf16(64)
    ST = cv.bf16(128)
    decdt = cv.bf16(128)
    attT = cv.bf16(128)
    convacc = cv.f32(128)
    cbscr_bf = cv.bf16(128)
    srun_bf = cv.bf16(128)

    t1 = p.dma("sync", cwc[:, :], cw_i.ap(), inc="ld")
    t2 = p.dma("sync", cbc[:, :], cb_i.ap(), inc="ld")
    t3 = p.dma("sync", dcol16[:, 0:1], dtb_i.ap(), inc="ld",
               allow_slow_non_contiguous=True)
    t4 = p.dma("sync", dcol16[:, 1:2], al_i.ap(), inc="ld",
               allow_slow_non_contiguous=True)
    drow = p.sbuf("drow", [1, 16], F32)
    t5 = p.dma("sync", drow[0:1, :], dd_i.ap(), inc="ld")
    t6 = p.dma("sync", xnwc[:, :], xnw_i.ap().rearrange("c p -> p c"),
               inc="ld", allow_slow_non_contiguous=True)
    t7 = p.dma("sync", xnbc[:, :], xnb_i.ap().rearrange("c p -> p c"),
               inc="ld", allow_slow_non_contiguous=True)
    t8 = p.dma("gpsimd", eye_bf[:, :], id_i.ap(), inc="ldw")
    t9 = p.dma("sync", tri[:, :], tri_i.ap(), inc="ld")
    t9b = p.dma("sync", eye_f[:, :], id_i.ap(), inc="ld")
    i1 = p.op("vector", lambda e: e.memset(onesrow16[:, :], 1.0), inc="init")
    i2 = p.op("vector", lambda e: e.memset(negrow[:, :], -1.0), inc="init")
    a1 = p.op("scalar", lambda e: e.activation(dcol16[:, 1:2], dcol16[:, 1:2],
              AF.Exp), after=[t4], inc="init")
    a2 = p.op("vector", lambda e: e.tensor_scalar(dcol16[:, 1:2],
              dcol16[:, 1:2], -1.0, None, ALU.mult), after=[a1], inc="init")
    dhi_toks = []
    for h in range(16):
        b1 = p.op("tensor", lambda e, h=h: e.matmul(
            p.psum_banks[6][:, 0:1], p.ones_row128[0:1, :],
            drow[0:1, h:h + 1], start=True, stop=True),
            after=[t5, p.psum_last_read[6]], inc="mm")
        b2 = p.op("vector", lambda e, h=h: e.tensor_copy(
            Dcols[:, h:h + 1], p.psum_banks[6][:, 0:1]), after=[b1],
            inc="init")
        p.psum_last_read[6] = b2
        dhi_toks.append(b2)
    rn_toks = []
    for oc in range(34):
        ln1 = p.dma("gpsimd", wn[:, :], wN_i.ap()[oc * 128:(oc + 1) * 128, :],
                    after=[rn_toks[-1] if rn_toks else None], inc="ldw")
        n1 = p.op("scalar", lambda e: e.activation(sqs[:, 0:512],
                  wn[:, 0:512], AF.Square), after=[ln1], inc="sq")
        n1b = p.op("scalar", lambda e: e.activation(sqs[:, 512:1024],
                   wn[:, 512:1024], AF.Square), after=[n1], inc="sq")
        n2 = p.op("vector", lambda e, oc=oc: e.reduce_sum(
            rncol[:, oc:oc + 1], sqs[:, 0:1024], axis=AX.X),
            after=[n1b], inc="vv")
        n3 = p.op("scalar", lambda e, oc=oc: e.activation(
            rncol[:, oc:oc + 1], rncol[:, oc:oc + 1], AF.Sqrt),
            after=[n2], inc="sq")
        n4 = p.op("vector", lambda e, oc=oc: e.tensor_scalar(
            rncol[:, oc:oc + 1], rncol[:, oc:oc + 1], 1e-6, None, ALU.max),
            after=[n3], inc="vv")
        n5 = p.op("vector", lambda e, oc=oc: e.reciprocal(
            rncol[:, oc:oc + 1], rncol[:, oc:oc + 1]), after=[n4], inc="vv")
        n6 = p.op("vector", lambda e, oc=oc: e.tensor_scalar(
            rncol[:, oc:oc + 1], rncol[:, oc:oc + 1], 1024.0 ** -0.5, None,
            ALU.mult), after=[n5], inc="vv")
        rn_toks.append(n6)
    r0 = p.op("tensor", lambda e: None,
              after=init_toks + rn_toks + dhi_toks +
              [t1, t2, t3, t6, t7, t9, t9b, i1, i2, a2], inc="ph0")
    r0v = p.op("vector", lambda e: None, after=[r0], inc="ph0v")
    r0s = p.op("scalar", lambda e: None, after=[r0], inc="ph0s")

    tiles = [(0, 128)] + [(128 + 512 * k, 512) for k in range(4)]
    xb_free, u_free = None, None
    w_free = [None, None]
    z_st = []
    dtevs = []
    xpre_toks = []
    for ti, (ecol, twd) in enumerate(tiles):
        ld = p.dma("sync", xb_t.rearrange("p (c t) -> p c t", c=8)[:, :, 0:twd],
                   fm_ap(xb_i)[:, :, ecol:ecol + twd], after=[xb_free],
                   inc="ld")
        lt1 = emit_ln2(p, lambda c, twd=twd: xb_t[:, c * 512:c * 512 + twd],
                       8, twd, sqs, ld, name=f"mba{ti}",
                       out_f32_fn=lambda c, twd=twd:
                       u1_t[:, c * 512:c * 512 + twd],
                       scratch=(ln_stats, ln_tmp))
        lt1 = p.op("vector", lambda e: None, after=[lt1, u_free], inc="cp")
        lt2 = emit_ln2(p, lambda c, twd=twd: u1_t[:, c * 512:c * 512 + twd],
                       8, twd, sqs, lt1, name=f"mbb{ti}",
                       out_bf_fn=lambda c, twd=twd:
                       u2_t[:, c * 512:c * 512 + twd],
                       gamma_col=xnwc, beta_col=xnbc,
                       scratch=(ln_stats, ln_tmp))
        ph = p.op("tensor", lambda e: None, after=[lt2, r0], inc="phl")
        oc_list = (list(range(16, 34)) if ti > 0 else list(range(16, 33))) \
            + (list(range(16)) if ti > 0 else [])
        for wgi in range(0, len(oc_list), 2):
            grp = oc_list[wgi:wgi + 2]
            slot = (wgi // 2) % 2
            lw = p.dma("gpsimd",
                       wsb.rearrange("p (s k x) -> p s k x", s=2, k=8)
                       [:, slot, :, 0:128 * len(grp)],
                       fm_ap(wT_i)[:, :, grp[0] * 128:grp[0] * 128 +
                                   128 * len(grp)],
                       after=[w_free[slot]], inc="ldw")
            for gi, oc in enumerate(grp):
                bank = (wgi + gi) % 4
                ps = p.psum_banks[bank]
                mm = None
                for kc in range(8):
                    mm = p.op("tensor", lambda e, kc=kc, gi=gi, twd=twd,
                              slot=slot, ps=ps: e.matmul(
                        ps[:, 0:twd],
                        wsb.rearrange("p (s k c x) -> p s k c x", s=2, k=8,
                                      c=2)[:, slot, kc, gi, :],
                        u2_t.rearrange("p (c t) -> p c t", c=8)[:, kc, 0:twd],
                        start=(kc == 0), stop=(kc == 7)),
                        after=([ph, lw, p.psum_last_read[bank]])
                        if kc == 0 else (), inc="mm" if kc == 7 else None)
                grp_last_mm = mm
                if oc < 16:
                    ev = p.op("vector", lambda e, oc=oc, twd=twd, ps=ps:
                        e.tensor_scalar(zbuf[:, 0:twd],
                                        ps[:, 0:twd], rncol[:, oc:oc + 1],
                                        None, ALU.mult),
                        after=[mm, z_st[-1] if z_st else None, r0v], inc="vv")
                    stz = p.dma("sync",
                                z_o.ap()[oc * 128:(oc + 1) * 128,
                                         ecol - 128:ecol - 128 + twd],
                                zbuf[:, 0:twd], after=[ev], inc="stz")
                    z_st.append(stz)
                elif oc < 33:
                    if ti == 0:
                        dst = xpre[:, (oc - 16) * 2052: (oc - 16) * 2052 + 4]
                        ev = p.op("vector", lambda e, dst=dst, ps=ps:
                            e.tensor_scalar(dst, ps[:, 124:128],
                                            rncol[:, oc:oc + 1], None,
                                            ALU.mult),
                            after=[mm, r0v], inc="xp")
                    else:
                        dst = xpre[:, (oc - 16) * 2052 + ecol - 124:
                                   (oc - 16) * 2052 + ecol - 124 + twd]
                        ev = p.op("vector", lambda e, dst=dst, twd=twd, ps=ps:
                            e.tensor_scalar(dst, ps[:, 0:twd],
                                            rncol[:, oc:oc + 1], None,
                                            ALU.mult),
                            after=[mm, r0v], inc="xp")
                    xpre_toks.append(ev)
                elif oc >= 33:
                    ev = p.op("vector", lambda e, twd=twd, ecol=ecol, ps=ps:
                        e.tensor_scalar(
                        dtrow[0:16, ecol - 128:ecol - 128 + twd],
                        ps[0:16, 0:twd], rncol[0:16, 33:34], None, ALU.mult),
                        after=[mm, r0v], inc="dt")
                    dtevs.append(ev)
                p.psum_last_read[bank] = ev
            w_free[slot] = grp_last_mm
        xb_free = lt2
        u_free = lt2

    ph1 = p.op("tensor", lambda e: None, after=xpre_toks + dtevs, inc="ph1t")
    if stage == 1:
        p.barrier_wait("sync", z_st + xpre_toks + dtevs)
        return p.build()
    ph1v = p.op("vector", lambda e: None, after=xpre_toks + dtevs + z_st,
                inc="ph1v")
    ph1s = p.op("scalar", lambda e: None, after=xpre_toks + dtevs, inc="ph1s")

    L0, L1, L2, L3 = 0, T, 2 * T, 3 * T
    L4 = L3
    d0 = p.op("vector", lambda e: e.tensor_scalar(
        dtrow[0:16, L0:L0 + T], dtrow[0:16, L0:L0 + T], dcol16[:, 0:1],
        None, ALU.add), after=[ph1v], inc="dv")
    d1 = p.op("vector", lambda e: e.tensor_scalar(
        dtrow[0:16, L1:L1 + T], dtrow[0:16, L0:L0 + T], 0.0, None, ALU.max),
        after=[d0], inc="dv")
    d2 = p.op("vector", lambda e: e.tensor_scalar(
        dtrow[0:16, L2:L2 + T], dtrow[0:16, L0:L0 + T], 0.0, None, ALU.min),
        after=[d0], inc="dv")
    d3 = p.op("vector", lambda e: e.tensor_sub(
        dtrow[0:16, L2:L2 + T], dtrow[0:16, L1:L1 + T],
        dtrow[0:16, L2:L2 + T]), after=[d1, d2], inc="dv")
    d4 = p.op("scalar", lambda e: e.activation(
        dtrow[0:16, L3:L3 + T], dtrow[0:16, L2:L2 + T], AF.Exp,
        scale=-1.0), after=[d3, ph1s], inc="ds")
    d5 = p.op("scalar", lambda e: e.activation(
        dtrow[0:16, L3:L3 + T], dtrow[0:16, L3:L3 + T], AF.Ln,
        bias=p.ones_col128[0:16, 0:1]), after=[d4], inc="ds")
    d6 = p.op("vector", lambda e: e.tensor_add(
        dtrow[0:16, L1:L1 + T], dtrow[0:16, L1:L1 + T],
        dtrow[0:16, L3:L3 + T]), after=[d5], inc="dv")
    d7 = p.op("scalar", lambda e: e.activation(
        dtrow[0:16, L2:L2 + T], dtrow[0:16, L1:L1 + T], AF.Ln),
        after=[d6], inc="ds")
    d8 = p.op("vector", lambda e: e.tensor_scalar(
        dtrow[0:16, L0:L0 + T], dtrow[0:16, L1:L1 + T], dcol16[:, 1:2],
        None, ALU.mult), after=[d6], inc="dv")
    d9 = p.op("vector", lambda e: e.tensor_tensor_scan(
        dtrow[0:16, L3:L3 + T], onesrow16[0:16, :],
        dtrow[0:16, L0:L0 + T], 0.0, ALU.mult, ALU.add),
        after=[d8, d5], inc="dv")
    ac_st = p.dma("sync", ac_o.ap(), dtrow[0:16, L3 + T - 1:L3 + T],
                  after=[d9], inc="sta", allow_slow_non_contiguous=True)
    prev_dl = d9
    for c in range(15, 0, -1):
        dl = p.op("vector", lambda e, c=c: e.tensor_scalar(
            dtrow[0:16, L3 + c * 128:L3 + (c + 1) * 128],
            dtrow[0:16, L3 + c * 128:L3 + (c + 1) * 128],
            dtrow[0:16, L3 + c * 128 - 1:L3 + c * 128], None,
            ALU.subtract), after=[prev_dl, ac_st], inc="dv")
        prev_dl = dl
    ph3 = p.op("tensor", lambda e: None, after=[prev_dl, d7, ph1], inc="ph3")
    ph3v = p.op("vector", lambda e: None, after=[ph3], inc="ph3v")
    ph3s = p.op("scalar", lambda e: None, after=[ph3], inc="ph3s")
    z0 = p.op("vector", lambda e: e.memset(Srun[:, :], 0.0), after=[ph3v],
              inc="dv")
    prow_tok = p.op("vector", lambda e: e.memset(prow[0:1, :], 0.0),
                    after=[ph3v], inc="dv")
    if stage == 2:
        p.barrier_wait("sync", z_st + [z0, prow_tok, ac_st])
        return p.build()

    y_st, c_st = [], []
    cumT_free = ph3s
    ybuf_free, cbuf_free = ph3v, ph3v
    scan_prev = z0
    conv_prev = [ph3s] * 17
    chunk_gate = ph3
    for c in range(nch):
        ccol = c * 128
        ecol = 128 + ccol
        conv_toks = []
        for cc in range(17):
            base = cc * 2052 + 1 + ccol
            c1 = p.op("vector", lambda e, cc=cc, base=base: e.tensor_scalar(
                convacc[:, 0:128],
                xpre[:, base: base + 128],
                cwc[:, cc * 4:cc * 4 + 1], None, ALU.mult),
                after=[ph1v, conv_prev[cc], chunk_gate], inc="cv")
            for k in (1, 2, 3):
                c1 = p.op("vector", lambda e, cc=cc, base=base, k=k:
                    e.scalar_tensor_tensor(
                    convacc[:, 0:128],
                    xpre[:, base + k: base + k + 128],
                    cwc[:, cc * 4 + k:cc * 4 + k + 1],
                    convacc[:, 0:128], ALU.mult, ALU.add),
                    after=[c1], inc="cv")
            if cc < 16:
                c2 = p.op("scalar", lambda e, cc=cc: e.activation(
                    xcfc[:, cc * 128:(cc + 1) * 128], convacc[:, 0:128],
                    AF.Silu, bias=cbc[:, cc:cc + 1]),
                    after=[c1, ph3s, conv_prev[cc]], inc="cs")
            else:
                c2a = p.op("scalar", lambda e, cc=cc: e.activation(
                    Bb[0:64, 0:128], convacc[0:64, 0:128], AF.Silu,
                    bias=cbc[0:64, cc:cc + 1]),
                    after=[c1, ph3s, conv_prev[cc]], inc="cs")
                c2 = p.op("scalar", lambda e, cc=cc: e.activation(
                    Cb[0:64, 0:128], convacc[64:128, 0:128], AF.Silu,
                    bias=cbc[64:128, cc:cc + 1]), after=[c1, c2a], inc="cs")
            conv_toks.append(c2)
            conv_prev[cc] = c2
        if stage == 3 and c == 0:
            p.barrier_wait("sync", conv_toks + z_st)
            return p.build()
        e1 = p.op("scalar", lambda e, ccol=ccol: e.activation(
            sqs[0:16, 0:128], dtrow[0:16, L4 + ccol:L4 + ccol + 128],
            AF.Exp, scale=-1.0,
            bias=dtrow[0:16, L4 + ccol + 127:L4 + ccol + 128]),
            after=[ph3s, cumT_free], inc="ms")
        e2 = p.op("vector", lambda e, ccol=ccol: e.tensor_mul(
            sqs[0:16, 0:128], sqs[0:16, 0:128],
            dtrow[0:16, L1 + ccol:L1 + ccol + 128]), after=[e1, ph3v],
            inc="mv")

        tr1 = p.op("tensor", lambda e, ccol=ccol: e.transpose(
            p.psum_banks[6][0:128, 0:16],
            dtrow[0:16, L2 + ccol:L2 + ccol + 128], eye_f[0:16, 0:16]),
            after=[ph3, p.psum_last_read[6], e2], inc="mm")
        tv1 = p.op("vector", lambda e: e.tensor_copy(
            cumT[:, 0:16], p.psum_banks[6][0:128, 0:16]), after=[tr1],
            inc="mv")
        tr2 = p.op("tensor", lambda e: e.transpose(
            p.psum_banks[6][0:128, 16:32], sqs[0:16, 0:128],
            eye_f[0:16, 0:16]), after=[tv1, e2], inc="mm")
        tv2 = p.op("vector", lambda e: e.tensor_copy(
            cumT[:, 16:32], p.psum_banks[6][0:128, 16:32]), after=[tr2],
            inc="mv")
        tr3 = p.op("tensor", lambda e, ccol=ccol: e.transpose(
            p.psum_banks[6][0:128, 32:48],
            dtrow[0:16, L4 + ccol:L4 + ccol + 128], eye_f[0:16, 0:16]),
            after=[tv2], inc="mm")
        tv3 = p.op("vector", lambda e: e.tensor_copy(
            cumT[:, 0:16] if False else ln_tmp[:, 0:16],
            p.psum_banks[6][0:128, 32:48]), after=[tr3], inc="mv")
        p.psum_last_read[6] = tv3
        row_toks = []
        for h in range(16):
            trh = p.op("tensor", lambda e, h=h: e.transpose(
                p.psum_banks[6][0:1, 0:128],
                ln_tmp[:, h:h + 1], eye_f[:, :]),
                after=[tv3, p.psum_last_read[6]], inc="mm")
            tvh = p.op("vector", lambda e, h=h: e.tensor_copy(
                cumrow[0:1, h * 128:(h + 1) * 128],
                p.psum_banks[6][0:1, 0:128]), after=[trh], inc="mv")
            p.psum_last_read[6] = tvh
            row_toks.append(tvh)
        exr = p.op("scalar", lambda e: e.activation(
            expcrow[0:1, :], cumrow[0:1, :], AF.Exp), after=row_toks,
            inc="ms")
        if glb >= 1:
            ep = p.op("scalar", lambda e: e.activation(
                eprow[0:1, :], prow[0:1, :], AF.Exp), after=[prow_tok, exr],
                inc="ms")
            tsg_toks = []
            for h in range(16):
                tsg_toks.append(p.op("vector", lambda e, h=h: e.tensor_scalar(
                    expg_bf[0:1, h * 128:(h + 1) * 128],
                    expcrow[0:1, h * 128:(h + 1) * 128],
                    eprow[0:1, h:h + 1], None, ALU.mult),
                    after=[exr, ep], inc="mv"))
            prow_tok = p.op("vector", lambda e: e.tensor_add(
                prow[0:1, :], prow[0:1, :],
                cumrow.ap().rearrange("o (h t) -> o h t", h=16)[0:1, :, 127]),
                after=tsg_toks + [ep], inc="mv")
        if stage == 4 and c == 0:
            p.barrier_wait("sync", row_toks + [exr, prow_tok] + z_st)
            return p.build()
        cd1 = p.op("tensor", lambda e: e.matmul(
            p.psum_banks[7][0:64, 0:16], p.ones_row128[0:1, 0:64],
            expcrow.ap().rearrange("o (h t) -> o h t", h=16)[0:1, :, 127],
            start=True, stop=True), after=[exr, p.psum_last_read[7]],
            inc="mm")
        cd2 = p.op("vector", lambda e: e.tensor_copy(
            cdc[:, :], p.psum_banks[7][0:64, 0:16]), after=[cd1], inc="mv")
        p.psum_last_read[7] = cd2
        cumT_free = tv2
        xct_toks = []
        for hh in range(16):
            trx = p.op("tensor", lambda e, hh=hh: e.transpose(
                p.pbf(3)[0:128, 0:128], xcfc[:, hh * 128:(hh + 1) * 128],
                eye_bf[:, :]),
                after=[p.psum_last_read[3], conv_toks[hh]], inc="mm")
            tvx = p.op("vector", lambda e, hh=hh: e.tensor_copy(
                xcT[:, hh * 128:(hh + 1) * 128], p.pbf(3)[0:128, 0:128]),
                after=[trx], inc="mv")
            p.psum_last_read[3] = tvx
            xct_toks.append(tvx)
        trb = p.op("tensor", lambda e: e.transpose(
            p.pbf(3)[0:128, 0:64], Bb[0:64, 0:128], eye_bf[0:64, 0:64]),
            after=[p.psum_last_read[3], conv_toks[16]], inc="mm")
        tvb = p.op("vector", lambda e: e.tensor_copy(
            Btok[:, 0:64], p.pbf(3)[0:128, 0:64]), after=[trb], inc="mv")
        p.psum_last_read[3] = tvb
        ms = p.op("tensor", lambda e: e.matmul(
            p.psum_banks[2][0:128, 0:128], Bb[0:64, 0:128],
            Cb[0:64, 0:128], start=True, stop=True),
            after=[p.psum_last_read[2], conv_toks[16]], inc="mm")
        msv = p.op("vector", lambda e: e.tensor_copy(
            ST[:, :], p.psum_banks[2][0:128, 0:128]), after=[ms], inc="mv")
        p.psum_last_read[2] = msv
        if stage == 5 and c == 0:
            p.barrier_wait("sync", [msv, tvb, cd2] + xct_toks + z_st)
            return p.build()
        hb_toks = []
        prev_g5 = prev_g7 = prev_y3 = None
        for h in range(hn):
            bank = h % 2
            psA = p.psum_banks[bank]
            g1 = p.op("tensor", lambda e, h=h, psA=psA: e.matmul(
                psA[:, 0:128], p.ones_row128[0:1, :],
                cumrow[0:1, h * 128:(h + 1) * 128],
                start=True, stop=False),
                after=[p.psum_last_read[bank], ph3, row_toks[h]], inc=None)
            g2 = p.op("tensor", lambda e, h=h, psA=psA: e.matmul(
                psA[:, 0:128],
                cumrow[0:1, h * 128:(h + 1) * 128],
                negrow[0:1, :], start=False, stop=True), inc="mm")
            g3 = p.op("vector", lambda e, psA=psA: e.tensor_add(
                psA[:, 0:128], psA[:, 0:128], tri[:, :]), after=[g2],
                inc="mv")
            g4 = p.op("scalar", lambda e, h=h, psA=psA: e.activation(
                decdt[:, :], psA[:, 0:128], AF.Exp, bias=cumT[:, h:h + 1]),
                after=[g3, tv1, prev_g5], inc="ms")
            p.psum_last_read[bank] = g4
            if hcut == 1 and h == hn - 1:
                p.barrier_wait("sync", [g4] + z_st)
                return p.build()
            g5 = p.op("vector", lambda e: e.tensor_mul(
                attT[:, :], decdt[:, :], ST[:, :]),
                after=[g4, msv, prev_y3], inc="mv")
            if c == 0 and h == 0:
                p.dma("sync", xc_o.ap(), xcfc[:, :], after=conv_toks,
                      inc="std")
                p.dma("sync", xp_o.ap(), xpre[:, 0:300], after=conv_toks,
                      inc="std")
                p.dma("sync", rn_o.ap(), rncol[:, :], after=conv_toks,
                      inc="std")
                p.dma("sync", att_o.ap()[:, 0:128], attT[:, :], after=[g5],
                      inc="std")
                p.dma("sync", att_o.ap()[:, 128:256], decdt[:, :],
                      after=[g5], inc="std")
                p.dma("sync", att_o.ap()[:, 256:384], ST[:, :], after=[g5],
                      inc="std")
            g6 = p.op("vector", lambda e, h=h: e.tensor_scalar(
                Bsc[:, 0:64], Btok[:, 0:64], cumT[:, 16 + h:17 + h], None,
                ALU.mult), after=[tvb, tv2, g5, prev_g7], inc="mv")
            g7 = p.op("tensor", lambda e, h=h: e.matmul(
                p.psum_banks[5][0:64, 0:128], Bsc[:, 0:64],
                xcT[:, h * 128:(h + 1) * 128], start=True, stop=True),
                after=[g6, xct_toks[h], p.psum_last_read[5]], inc="mm")
            if hcut == 2 and h == hn - 1:
                p.barrier_wait("sync", [g7, g5, g6] + z_st)
                return p.build()
            g8 = p.op("tensor", lambda e, h=h: e.matmul(
                p.psum_banks[7][0:64, 0:128], p.ones_row128[0:1, 0:64],
                expcrow[0:1, h * 128:(h + 1) * 128], start=True, stop=True),
                after=[exr, p.psum_last_read[7]], inc="mm")
            g9 = p.op("vector", lambda e: e.tensor_mul(
                cbscr_bf[0:64, 0:128], p.psum_banks[7][0:64, 0:128],
                Cb[0:64, 0:128]), after=[g8, conv_toks[16], prev_y3],
                inc="mv")
            p.psum_last_read[7] = g9
            if glb >= 2:
                g8g = p.op("tensor", lambda e, h=h: e.matmul(
                    p.psum_banks[7][0:64, 128:256], p.ones_row_bf[0:1, 0:64],
                    expg_bf[0:1, h * 128:(h + 1) * 128], start=True,
                    stop=True),
                    after=[tsg_toks[h], p.psum_last_read[7]], inc="mm")
                g9g = p.op("vector", lambda e: e.tensor_mul(
                    cbg[0:64, 0:128], p.psum_banks[7][0:64, 128:256],
                    Cb[0:64, 0:128]), after=[g8g, conv_toks[16]], inc="mv")
                p.psum_last_read[7] = g9g
                g11 = p.op("vector", lambda e, h=h: e.tensor_copy(
                    cbuf[(h % 2) * 64:(h % 2) * 64 + 64,
                         (h // 2) * 128:(h // 2) * 128 + 128],
                    cbg[0:64, 0:128]), after=[g9g, cbuf_free], inc="mv")
            else:
                g11 = p.op("vector", lambda e, h=h: e.tensor_copy(
                    cbuf[(h % 2) * 64:(h % 2) * 64 + 64,
                         (h // 2) * 128:(h // 2) * 128 + 128],
                    cbscr[0:64, 0:128]), after=[g9, cbuf_free], inc="mv")
            if hcut == 3 and h == hn - 1:
                p.barrier_wait("sync", [g11] + z_st)
                return p.build()
            y1 = p.op("tensor", lambda e, h=h: e.matmul(
                p.psum_banks[4][:, 0:128], xcT[:, h * 128:(h + 1) * 128],
                attT[:, :], start=True, stop=True),
                after=[g5, xct_toks[h], p.psum_last_read[4]], inc=None)
            sbc = p.op("vector", lambda e, h=h: e.tensor_copy(
                srun_bf[0:64, 0:128], Srun[:, h * 128:(h + 1) * 128]),
                after=[scan_prev, prev_y3], inc="mv")
            y2 = p.op("tensor", lambda e, h=h: e.matmul(
                p.psum_banks[6][:, 256:384], srun_bf[0:64, 0:128],
                cbscr_bf[0:64, 0:128], start=True, stop=True),
                after=[g9, sbc, p.psum_last_read[6]], inc=None)
            y3 = p.op("tensor", lambda e: None, after=[y2], inc="mm")
            y4 = p.op("vector", lambda e, h=h: e.scalar_tensor_tensor(
                ybuf[:, h * 128:(h + 1) * 128],
                xcfc[:, h * 128:(h + 1) * 128], Dcols[:, h:h + 1],
                p.psum_banks[4][:, 0:128], ALU.mult, ALU.add),
                after=[y3, ybuf_free, conv_toks[h]], inc="mv")
            p.psum_last_read[4] = y4
            y4b = p.op("vector", lambda e, h=h: e.tensor_add(
                ybuf[:, h * 128:(h + 1) * 128],
                ybuf[:, h * 128:(h + 1) * 128],
                p.psum_banks[6][:, 256:384]), after=[y4], inc="mv")
            p.psum_last_read[6] = y4b
            if hcut == 4 and h == hn - 1:
                p.barrier_wait("sync", [y4b] + z_st)
                return p.build()
            s1 = p.op("vector", lambda e, h=h: e.scalar_tensor_tensor(
                Srun[:, h * 128:(h + 1) * 128],
                Srun[:, h * 128:(h + 1) * 128], cdc[:, h:h + 1],
                p.psum_banks[5][0:64, 0:128], ALU.mult, ALU.add),
                after=[g7, y3, cd2], inc="mv")
            p.psum_last_read[5] = s1
            scan_prev = s1
            prev_g5, prev_g7, prev_y3 = g5, g7, y3
            hb_toks.extend([y4b, g11])
        if stage == 6 and c == 0:
            p.barrier_wait("sync", hb_toks + z_st)
            return p.build()
        sty = p.dma("sync", y_o.ap().rearrange("(c p) t -> p c t", p=128)
                    [:, :, ccol:ccol + 128],
                    ybuf.rearrange("p (c t) -> p c t", c=16), after=hb_toks,
                    inc="sty")
        stc = p.dma("sync", ch_o.ap().rearrange("(c p) t -> p c t", p=128)
                    [:, :, ccol:ccol + 128],
                    cbuf.rearrange("p (c t) -> p c t", c=8), after=hb_toks,
                    inc="stc")
        ybuf_free, cbuf_free = sty, stc
        y_st.append(sty)
        c_st.append(stc)
        chunk_gate = p.op("tensor", lambda e: None,
                          after=hb_toks + [prev_y3, msv, tvb], inc="mm")
    sts = p.dma("sync", sc_o.ap(), Srun[:, :], after=[scan_prev], inc="sts")
    p.barrier_wait("sync", y_st + c_st + z_st + [sts, ac_st] +
                   ([("std", p.sem_plan.get("std", 0))]
                    if "std" in p.sem_plan else []))
    return p.build()



def build_final(ntiles=4, nexp=18, dbg=False):
    p = Prog("fin")
    y_i = p.dram("y_locT", [2048, T], BF16, kind="ExternalInput")
    ch_i = p.dram("chT", [1024, T], BF16, kind="ExternalInput")
    z_i = p.dram("zT", [2048, T], BF16, kind="ExternalInput")
    sp_i = p.dram("S_prev", [7 * 64, 2048], F32, kind="ExternalInput")
    ap_i = p.dram("A_prev", [1, 112], F32, kind="ExternalInput")
    xb_i = p.dram("xbT", [D, T], F32, kind="ExternalInput")
    rw_i = p.dram("rmsw", [128, 16], F32, kind="ExternalInput")
    opT_i = p.dram("outprojT", [2048, D], F32, kind="ExternalInput")
    opN_i = p.dram("outprojN", [D, 2048], F32, kind="ExternalInput")
    gw_i = p.dram("gateW", [D, 16], F32, kind="ExternalInput")
    gb_i = p.dram("gateB", [1, 16], F32, kind="ExternalInput")
    wg_i = p.dram("wg_all", [nexp * D, 512], F32, kind="ExternalInput")
    wu_i = p.dram("wu_all", [nexp * D, 512], F32, kind="ExternalInput")
    wd_i = p.dram("wd_all", [nexp * 512, D], F32, kind="ExternalInput")
    id_i = p.dram("ident128", [128, 128], F32, kind="ExternalInput")
    out_o = p.dram("out", [D, T], F32, kind="ExternalOutput")
    if dbg:
        xc7_o = p.dram("xc7", [128, 8 * 512], F32, kind="ExternalOutput")
        g7_o = p.dram("g7", [128, 16 * 512], BF16, kind="ExternalOutput")
        x47_o = p.dram("x47", [128, 8 * 512], BF16, kind="ExternalOutput")
        cmb7_o = p.dram("cmb7", [128, 64], F32, kind="ExternalOutput")
        rn7_o = p.dram("rn7", [128, 8], F32, kind="ExternalOutput")
        rb7_o = p.dram("rb7", [128, 512], F32, kind="ExternalOutput")
        moe7_o = p.dram("moe7", [128, 8 * 512], F32, kind="ExternalOutput")
        scg7_o = p.dram("scg7", [128, 37], F32, kind="ExternalOutput")

    p.psum_init()
    init_toks = p.init_ones()
    wproj = p.sbuf("wproj", [128, 16 * 1024], BF16)
    ylt = p.sbuf("ylt", [128, 16 * 512], BF16)
    gbuf = p.sbuf("gbuf", [128, 16 * 512], BF16)
    xc_t = p.sbuf("xc_t", [128, 8 * 512], F32)
    mscr = p.sbuf("mscr", [128, 8 * 512], F32)
    x4h = p.sbuf("x4h", [128, 8 * 512], BF16)
    sqs = p.sbuf("sqs", [128, 8 * 512], F32)
    wgsb = p.sbuf("wgsb", [128, 2 * 8 * 512], BF16)
    wusb = p.sbuf("wusb", [128, 2 * 8 * 512], BF16)
    wdsb = p.sbuf("wdsb", [128, 2 * 4 * 1024], BF16)
    stin = p.sbuf("stin", [64, 16 * 128], F32)
    stin_bf = p.sbuf("stinb", [64, 16 * 128], BF16)
    chtl = p.sbuf("chtl", [64, 2 * 512], BF16)
    glb = p.sbuf("glb", [128, 4 * 256], BF16)
    ubuf = p.sbuf("ubuf", [128, 4 * 256], BF16)
    outsm = p.sbuf("outsm", [128, 512], F32)
    ln_st = p.sbuf("ln_st", [1, 3 * 512], F32)
    srow = p.sbuf("srow", [1, 512], F32)
    rbs = p.sbuf("rbs", [128, 512], F32)
    scg = p.sbuf("scg", [128, 64], F32)
    cmbt4 = p.sbuf("cmbt4", [128, 4 * 16], F32)
    crow = p.sbuf("crow", [1, 256], F32)
    cbsb = p.sbuf("cbsb", [128, 256], BF16)
    arow = p.sbuf("arow", [1, 112], F32)
    sfxr = p.sbuf("sfxr", [1, 112], F32)
    wcol = p.sbuf("wcol", [64, 16], F32)
    gksb = p.sbuf("gksb", [128, 8 * 16], F32)
    negcs = p.sbuf("negcs", [1, 16], F32)
    rcol = p.sbuf("rcol", [128, 1], F32)
    grow = p.sbuf("grow", [1, 16], F32)
    gbb = p.sbuf("gbb", [128, 16], F32)
    rmsw = p.sbuf("rmsw_sb", [128, 16], F32)
    rnc = p.sbuf("rnc", [128, 8], F32)
    eye_f = p.sbuf("eyef", [128, 128], F32)

    ta = p.dma("sync", arow[0:1, :], ap_i.ap(), inc="ld")
    tg = p.dma("sync", gksb.ap().rearrange("p (c e) -> p c e", c=8),
               gw_i.ap().rearrange("(c p) e -> p c e", p=128), inc="ld")
    tb = p.dma("sync", grow[0:1, :], gb_i.ap(), inc="ld")
    trw = p.dma("sync", rmsw[:, :], rw_i.ap(), inc="ld")
    tid = p.dma("sync", eye_f[:, :], id_i.ap(), inc="ld")
    gb1 = p.op("tensor", lambda e: e.matmul(
        p.psum_banks[6][:, 0:16], p.ones_row128[0:1, :], grow[0:1, :],
        start=True, stop=True), after=[tb] + init_toks, inc="mm")
    gb2 = p.op("vector", lambda e: e.tensor_copy(gbb[:, :],
               p.psum_banks[6][:, 0:16]), after=[gb1], inc="ev")
    p.psum_last_read[6] = gb2
    ncs1 = None
    for c8 in range(8):
        ncs1 = p.op("tensor", lambda e, c8=c8: e.matmul(
            p.psum_banks[6][0:1, 16:32], p.ones_col128[:, 0:1],
            gksb[:, c8 * 16:(c8 + 1) * 16], start=(c8 == 0), stop=(c8 == 7)),
            after=([tg, gb2] + init_toks) if c8 == 0 else (),
            inc="mm" if c8 == 7 else None)
    ncs2 = p.op("vector", lambda e: e.tensor_scalar(
        negcs[0:1, :], p.psum_banks[6][0:1, 16:32], -1.0, None, ALU.mult),
        after=[ncs1], inc="ev")
    p.psum_last_read[6] = ncs2
    rn_toks = []
    prevw = None
    for oc in range(8):
        lw = p.dma("gpsimd", gbuf[:, 0:2048],
                   opN_i.ap()[oc * 128:(oc + 1) * 128, :],
                   after=[prevw], inc="ldw")
        n1 = p.op("scalar", lambda e: e.activation(sqs[:, 0:1024],
                  gbuf[:, 0:1024], AF.Square), after=[lw], inc="sq")
        n1b = p.op("scalar", lambda e: e.activation(sqs[:, 1024:2048],
                   gbuf[:, 1024:2048], AF.Square), after=[n1], inc="sq")
        n2 = p.op("vector", lambda e, oc=oc: e.reduce_sum(
            rnc[:, oc:oc + 1], sqs[:, 0:2048], axis=AX.X), after=[n1b],
            inc="ev")
        n3 = p.op("scalar", lambda e, oc=oc: e.activation(
            rnc[:, oc:oc + 1], rnc[:, oc:oc + 1], AF.Sqrt), after=[n2],
            inc="sq")
        n4 = p.op("vector", lambda e, oc=oc: e.tensor_scalar(
            rnc[:, oc:oc + 1], rnc[:, oc:oc + 1], 1e-6, None, ALU.max),
            after=[n3], inc="ev")
        n5 = p.op("vector", lambda e, oc=oc: e.reciprocal(
            rnc[:, oc:oc + 1], rnc[:, oc:oc + 1]), after=[n4], inc="ev")
        prevw = n5
        rn_toks.append(n5)
    twp = p.dma("gpsimd", fm_sb(wproj, 16, 1024), fm_ap(opT_i),
                after=[prevw], inc="ldw")
    wp_toks = []
    for ch in range(16):
        wp_toks.append(p.op("vector", lambda e, ch=ch: e.tensor_scalar(
            wproj[:, ch * 1024:(ch + 1) * 1024],
            wproj[:, ch * 1024:(ch + 1) * 1024], rmsw[:, ch:ch + 1], None,
            ALU.mult), after=[twp, trw], inc="ev"))
    sz = p.op("vector", lambda e: e.memset(sfxr[0:1, 96:112], 0.0),
              after=[ta], inc="ev")
    prev = sz
    for j in range(5, -1, -1):
        prev = p.op("vector", lambda e, j=j: e.tensor_add(
            sfxr[0:1, j * 16:(j + 1) * 16],
            arow[0:1, (j + 1) * 16:(j + 2) * 16],
            sfxr[0:1, (j + 1) * 16:(j + 2) * 16]), after=[prev], inc="ev")
    sfe = p.op("scalar", lambda e: e.activation(sfxr[0:1, :], sfxr[0:1, :],
               AF.Exp), after=[prev], inc="sq")
    z0 = p.op("vector", lambda e: e.memset(stin[:, :], 0.0), inc="ev")
    acc_prev = z0
    for j in range(7):
        lsp = p.dma("sync", mscr[0:64, 0:2048],
                    sp_i.ap()[j * 64:(j + 1) * 64, :],
                    after=[acc_prev], inc="ld")
        wb1 = p.op("tensor", lambda e, j=j: e.matmul(
            p.psum_banks[6][0:64, 0:16], p.ones_row128[0:1, 0:64],
            sfxr[0:1, j * 16:(j + 1) * 16], start=True, stop=True),
            after=[sfe, p.psum_last_read[6]], inc="mm")
        wb2 = p.op("vector", lambda e: e.tensor_copy(
            wcol[:, :], p.psum_banks[6][0:64, 0:16]), after=[wb1], inc="ev")
        p.psum_last_read[6] = wb2
        for h in range(16):
            acc_prev = p.op("vector", lambda e, h=h: e.scalar_tensor_tensor(
                stin[:, h * 128:(h + 1) * 128],
                mscr[0:64, h * 128:(h + 1) * 128],
                wcol[:, h:h + 1], stin[:, h * 128:(h + 1) * 128],
                ALU.mult, ALU.add), after=[lsp, wb2], inc="ev")
    sbf = p.op("vector", lambda e: e.tensor_copy(stin_bf[:, :], stin[:, :]),
               after=[acc_prev], inc="ev")
    pre_t = p.op("tensor", lambda e: None,
                 after=[sbf, gb2, ncs2, tid, tg] + wp_toks + rn_toks,
                 inc="pre")
    pre_v = p.op("vector", lambda e: None, after=[pre_t], inc="prev")
    pre_s = p.op("scalar", lambda e: None, after=[pre_t], inc="pres")

    out_dmas = []
    dbg_dmas = []
    tile_done = pre_v
    wg_free = [None, None]
    for tt in range(ntiles):
        t0 = tt * 512
        lyl = p.dma("sync", ylt.ap().rearrange("p (c t) -> p c t", c=16),
                    y_i.ap().rearrange("(c p) t -> p c t", p=128)
                    [:, :, t0:t0 + 512], after=[tile_done], inc="ld")
        lz = p.dma("sync", gbuf.ap().rearrange("p (c t) -> p c t", c=16),
                   z_i.ap().rearrange("(c p) t -> p c t", p=128)
                   [:, :, t0:t0 + 512], after=[tile_done], inc="ld")
        lxb = p.dma("sync", fm_sb(mscr, 8, 512),
                    fm_ap(xb_i)[:, :, t0:t0 + 512], after=[tile_done],
                    inc="ld")
        g_toks = []
        ch_free = [None, None]
        for h in range(16):
            slot = h % 2
            bank = h % 2
            ps = p.psum_banks[bank]
            lch = p.dma("sync", chtl[:, slot * 512:(slot + 1) * 512],
                        ch_i.ap()[h * 64:(h + 1) * 64, t0:t0 + 512],
                        after=[ch_free[slot], tile_done], inc="ld")
            m1 = p.op("tensor", lambda e, h=h, ps=ps, slot=slot: e.matmul(
                ps[:, 0:512], stin_bf[:, h * 128:(h + 1) * 128],
                chtl[:, slot * 512:(slot + 1) * 512], start=True, stop=True),
                after=[lch, pre_t, p.psum_last_read[bank]], inc="mm")
            ch_free[slot] = m1
            sz1 = p.op("scalar", lambda e, h=h: e.activation(
                gbuf[:, h * 512:(h + 1) * 512],
                gbuf[:, h * 512:(h + 1) * 512], AF.Silu),
                after=[lz, pre_s], inc="sq")
            m2 = p.op("vector", lambda e, h=h, ps=ps: e.tensor_add(
                ps[:, 0:512], ps[:, 0:512], ylt[:, h * 512:(h + 1) * 512]),
                after=[m1, lyl, pre_v], inc="ev")
            m3 = p.op("vector", lambda e, h=h, ps=ps: e.tensor_mul(
                gbuf[:, h * 512:(h + 1) * 512],
                gbuf[:, h * 512:(h + 1) * 512], ps[:, 0:512]),
                after=[m2, sz1], inc="ev")
            p.psum_last_read[bank] = m3
            g_toks.append(m3)
        sq_toks = []
        mm_toks = []
        for h in range(16):
            blk = h % 8
            dep = [g_toks[h]] + ([mm_toks[h - 8]] if h >= 8 else [])
            sq_toks.append(p.op("scalar", lambda e, h=h, blk=blk:
                e.activation(sqs[:, blk * 512:blk * 512 + 512],
                gbuf[:, h * 512:(h + 1) * 512], AF.Square), after=dep,
                inc="sq"))
            mm_toks.append(p.op("tensor", lambda e, h=h, blk=blk: e.matmul(
                p.psum_banks[2][0:1, 0:512], p.ones_col128[:, 0:1],
                sqs[:, blk * 512:blk * 512 + 512],
                start=(h == 0), stop=(h == 15)),
                after=[sq_toks[h]] + ([p.psum_last_read[2]] if h == 0
                                      else []), inc="mm"))
        r1 = p.op("vector", lambda e: e.tensor_scalar(
            srow[0:1, :], p.psum_banks[2][0:1, 0:512], 1.0 / 2048.0, 1e-5,
            ALU.mult, ALU.add), after=[mm_toks[-1]], inc="ev")
        p.psum_last_read[2] = r1
        r2 = p.op("scalar", lambda e: e.activation(srow[0:1, :],
                  srow[0:1, :], AF.Ln), after=[r1], inc="sq")
        r3 = p.op("scalar", lambda e: e.activation(srow[0:1, :],
                  srow[0:1, :], AF.Exp, scale=-0.5), after=[r2], inc="sq")
        r4 = p.op("tensor", lambda e: e.matmul(
            p.psum_banks[2][:, 0:512], p.ones_row128[0:1, :], srow[0:1, :],
            start=True, stop=True), after=[r3, p.psum_last_read[2]],
            inc="mm")
        r5 = p.op("vector", lambda e: e.tensor_copy(rbs[:, :],
                  p.psum_banks[2][:, 0:512]), after=[r4], inc="ev")
        p.psum_last_read[2] = r5
        xc_toks = []
        for oc in range(8):
            bank = 2 + oc % 2
            ps = p.psum_banks[bank]
            mm = None
            for kc in range(16):
                mm = p.op("tensor", lambda e, kc=kc, oc=oc, ps=ps: e.matmul(
                    ps[:, 0:512],
                    wproj[:, kc * 1024 + oc * 128:kc * 1024 + oc * 128 + 128],
                    gbuf[:, kc * 512:(kc + 1) * 512],
                    start=(kc == 0), stop=(kc == 15)),
                    after=(g_toks + [p.psum_last_read[bank], pre_t])
                    if kc == 0 else (), inc="mm" if kc == 15 else None)
            e1 = p.op("vector", lambda e, oc=oc, ps=ps: e.tensor_mul(
                xc_t[:, oc * 512:(oc + 1) * 512], ps[:, 0:512], rbs[:, :]),
                after=[mm, r5], inc="ev")
            e2 = p.op("vector", lambda e, oc=oc: e.tensor_scalar(
                xc_t[:, oc * 512:(oc + 1) * 512],
                xc_t[:, oc * 512:(oc + 1) * 512], rnc[:, oc:oc + 1], None,
                ALU.mult), after=[e1], inc="ev")
            e3 = p.op("vector", lambda e, oc=oc: e.tensor_add(
                xc_t[:, oc * 512:(oc + 1) * 512],
                xc_t[:, oc * 512:(oc + 1) * 512],
                mscr[:, oc * 512:(oc + 1) * 512]), after=[e2, lxb],
                inc="ev")
            p.psum_last_read[bank] = e1
            xc_toks.append(e3)
        xc_bar = p.op("vector", lambda e: None, after=xc_toks, inc="ev")
        if dbg and tt == 0:
            dbg_dmas.append(p.dma("sync", xc7_o.ap(), xc_t[:, :],
                                  after=[xc_bar], inc="std"))
            dbg_dmas.append(p.dma("sync", g7_o.ap(), gbuf[:, :],
                                  after=[xc_bar], inc="std"))
            dbg_dmas.append(p.dma("sync", rn7_o.ap(), rnc[:, :],
                                  after=[xc_bar], inc="std"))
            dbg_dmas.append(p.dma("sync", rb7_o.ap(), rbs[:, :],
                                  after=[xc_bar], inc="std"))
        ln_done = emit_ln2(p, lambda c: xc_t[:, c * 512:(c + 1) * 512], 8,
                           512, sqs, xc_bar, name=f"l4_{tt}",
                           out_bf_fn=lambda c: x4h[:, c * 512:(c + 1) * 512],
                           scratch=(ln_st, outsm))
        ln_t = p.op("tensor", lambda e: None, after=[ln_done], inc="mm")
        P0, G0, T0, M0, E0 = 16, 22, 26, 33, 37
        gt_toks = []
        sub_prev = None
        for sub in range(4):
            s0 = sub * 128
            dot = None
            for kc in range(8):
                dot = p.op("tensor", lambda e, kc=kc, s0=s0: e.matmul(
                    p.psum_banks[4][0:128, 0:16],
                    xc_t[:, kc * 512 + s0:kc * 512 + s0 + 128],
                    gksb[:, kc * 16:(kc + 1) * 16],
                    start=(kc == 0), stop=(kc == 7)),
                    after=([ln_t, p.psum_last_read[4]]) if kc == 0 else (),
                    inc=None)
            outer = p.op("tensor", lambda e, s0=s0: e.matmul(
                p.psum_banks[4][0:128, 16:32],
                ln_st[0:1, 512 + s0:512 + s0 + 128], negcs[0:1, :],
                start=True, stop=True), inc=None)
            trr = p.op("tensor", lambda e, s0=s0: e.transpose(
                p.psum_banks[4][0:128, 32:33],
                ln_st[0:1, 1024 + s0:1024 + s0 + 128], eye_f[0:1, 0:1]),
                inc="mm")
            evr = p.op("vector", lambda e: e.tensor_copy(
                rcol[:, :], p.psum_banks[4][0:128, 32:33]),
                after=[trr, sub_prev], inc="ev")
            s1t = p.op("vector", lambda e: e.tensor_scalar(
                scg[:, 0:16], p.psum_banks[4][0:128, 0:16], rcol[:, 0:1],
                None, ALU.mult), after=[evr], inc="ev")
            s2t = p.op("vector", lambda e: e.tensor_add(
                scg[:, 0:16], scg[:, 0:16], p.psum_banks[4][0:128, 16:32]),
                after=[s1t], inc="ev")
            p.psum_last_read[4] = s2t
            g1 = p.op("scalar", lambda e: e.activation(
                scg[:, 0:16], scg[:, 0:16], AF.Sigmoid),
                after=[s2t], inc="sq")
            g2 = p.op("vector", lambda e: e.tensor_add(
                scg[:, 0:16], scg[:, 0:16], gbb[:, :]), after=[g1, pre_v],
                inc="ev")
            last = g2
            for g in range(4):
                pairs = [(a, b) for a in range(4) for b in range(a + 1, 4)]
                for i, (a, b) in enumerate(pairs):
                    last = p.op("vector", lambda e, g=g, a=a, b=b, i=i:
                        e.tensor_add(scg[:, P0 + i:P0 + i + 1],
                        scg[:, 4 * g + a:4 * g + a + 1],
                        scg[:, 4 * g + b:4 * g + b + 1]), after=[last],
                        inc="ev")
                for (d, s1_, s2_) in ((0, P0, P0 + 1), (1, P0 + 2, P0 + 3),
                                      (2, P0 + 4, P0 + 5)):
                    last = p.op("vector", lambda e, d=d, s1_=s1_, s2_=s2_:
                        e.tensor_tensor(scg[:, T0 + d:T0 + d + 1],
                        scg[:, s1_:s1_ + 1], scg[:, s2_:s2_ + 1], ALU.max),
                        after=[last], inc="ev")
                last = p.op("vector", lambda e, g=g: e.tensor_tensor(
                    scg[:, G0 + g:G0 + g + 1], scg[:, T0:T0 + 1],
                    scg[:, T0 + 1:T0 + 2], ALU.max), after=[last], inc="ev")
                last = p.op("vector", lambda e, g=g: e.tensor_tensor(
                    scg[:, G0 + g:G0 + g + 1], scg[:, G0 + g:G0 + g + 1],
                    scg[:, T0 + 2:T0 + 3], ALU.max), after=[last], inc="ev")
            for (d, s1_, s2_, op) in (
                    (0, G0, G0 + 1, ALU.max), (1, G0 + 2, G0 + 3, ALU.max),
                    (2, G0, G0 + 1, ALU.min), (3, G0 + 2, G0 + 3, ALU.min)):
                last = p.op("vector", lambda e, d=d, s1_=s1_, s2_=s2_, op=op:
                    e.tensor_tensor(scg[:, T0 + d:T0 + d + 1],
                    scg[:, s1_:s1_ + 1], scg[:, s2_:s2_ + 1], op),
                    after=[last], inc="ev")
            last = p.op("vector", lambda e: e.tensor_tensor(
                scg[:, T0 + 4:T0 + 5], scg[:, T0:T0 + 1],
                scg[:, T0 + 1:T0 + 2], ALU.min), after=[last], inc="ev")
            last = p.op("vector", lambda e: e.tensor_tensor(
                scg[:, T0 + 5:T0 + 6], scg[:, T0 + 2:T0 + 3],
                scg[:, T0 + 3:T0 + 4], ALU.max), after=[last], inc="ev")
            last = p.op("vector", lambda e: e.tensor_tensor(
                scg[:, T0 + 6:T0 + 7], scg[:, T0 + 4:T0 + 5],
                scg[:, T0 + 5:T0 + 6], ALU.max), after=[last], inc="ev")
            for g in range(4):
                last = p.op("vector", lambda e, g=g: e.tensor_scalar(
                    scg[:, M0 + g:M0 + g + 1], scg[:, G0 + g:G0 + g + 1],
                    scg[:, T0 + 6:T0 + 7], None, ALU.is_ge), after=[last],
                    inc="ev")
            if dbg and tt == 0 and sub == 0:
                last = p.dma("sync", scg7_o.ap(), scg[:, 0:37], after=[last],
                             inc="std")
                dbg_dmas.append(last)
            for ee in range(16):
                last = p.op("vector", lambda e, ee=ee: e.tensor_copy(
                    scg[:, E0 + ee:E0 + ee + 1],
                    scg[:, M0 + ee // 4:M0 + ee // 4 + 1]), after=[last],
                    inc="ev")
            last = p.op("vector", lambda e: e.tensor_mul(
                scg[:, 0:16], scg[:, 0:16], scg[:, E0:E0 + 16]),
                after=[last], inc="ev")
            cmbs = cmbt4[:, sub * 16:(sub + 1) * 16]
            last = p.op("vector", lambda e, cmbs=cmbs: e.memset(cmbs, 0.0),
                        after=[last], inc="ev")
            last = p.op("vector", lambda e: e.memset(
                scg[:, T0 + 1:T0 + 2], 0.0), after=[last], inc="ev")
            for r in range(4):
                last = p.op("vector", lambda e: e.reduce_max(
                    scg[:, T0:T0 + 1], scg[:, 0:16], axis=AX.X),
                    after=[last], inc="ev")
                last = p.op("vector", lambda e: e.tensor_scalar(
                    scg[:, E0:E0 + 16], scg[:, 0:16], scg[:, T0:T0 + 1],
                    None, ALU.is_ge), after=[last], inc="ev")
                last = p.op("vector", lambda e, cmbs=cmbs:
                    e.scalar_tensor_tensor(
                    cmbs, scg[:, E0:E0 + 16], scg[:, T0:T0 + 1],
                    cmbs, ALU.mult, ALU.add), after=[last], inc="ev")
                last = p.op("vector", lambda e: e.tensor_add(
                    scg[:, T0 + 1:T0 + 2], scg[:, T0 + 1:T0 + 2],
                    scg[:, T0:T0 + 1]), after=[last], inc="ev")
                if r < 3:
                    last = p.op("vector", lambda e: e.scalar_tensor_tensor(
                        scg[:, 0:16], scg[:, E0:E0 + 16], -1e30,
                        scg[:, 0:16], ALU.mult, ALU.add), after=[last],
                        inc="ev")
            last = p.op("vector", lambda e: e.tensor_scalar(
                scg[:, T0 + 1:T0 + 2], scg[:, T0 + 1:T0 + 2], 1e-20, None,
                ALU.add), after=[last], inc="ev")
            last = p.op("vector", lambda e: e.reciprocal(
                scg[:, T0 + 1:T0 + 2], scg[:, T0 + 1:T0 + 2]), after=[last],
                inc="ev")
            last = p.op("vector", lambda e: e.tensor_scalar(
                scg[:, T0 + 1:T0 + 2], scg[:, T0 + 1:T0 + 2], 2.5, None,
                ALU.mult), after=[last], inc="ev")
            last = p.op("vector", lambda e, cmbs=cmbs: e.tensor_scalar(
                cmbs, cmbs, scg[:, T0 + 1:T0 + 2], None,
                ALU.mult), after=[last], inc="ev")
            gt_toks.append(last)
            sub_prev = last
        gt_bar = p.op("vector", lambda e: None, after=gt_toks, inc="ev")
        gt_bar_s = p.op("scalar", lambda e: None, after=[gt_bar], inc="sq")
        if dbg and tt == 0:
            dbg_dmas.append(p.dma("sync", x47_o.ap(), x4h[:, :],
                                  after=[ln_done], inc="std"))
            dbg_dmas.append(p.dma("sync", cmb7_o.ap(), cmbt4[:, :],
                                  after=[gt_bar], inc="std"))
        mz_toks = [p.op("vector", lambda e, b=b: e.memset(
            mscr[:, b * 1024:(b + 1) * 1024], 0.0), after=xc_toks + [gt_bar],
            inc="ev") for b in range(4)]
        acc_toks = {(stq, oc): mz_toks[(2 * oc + stq) // 4]
                    for stq in range(2) for oc in range(8)}
        dn_last = {}
        crow_free = None
        prev_dbar = None
        for en in range(nexp):
            slot = en % 2
            lwg = p.dma("gpsimd",
                        wgsb.ap().rearrange("p (s c m) -> p s c m",
                                            s=2, c=8)[:, slot],
                        wg_i.ap().rearrange("(n c p) m -> n p c m",
                                            n=nexp, p=128)[en],
                        after=[wg_free[slot]], inc="ldw")
            lwu = p.dma("gpsimd",
                        wusb.ap().rearrange("p (s c m) -> p s c m",
                                            s=2, c=8)[:, slot],
                        wu_i.ap().rearrange("(n c p) m -> n p c m",
                                            n=nexp, p=128)[en],
                        after=[wg_free[slot]], inc="ldw")
            lwd = p.dma("gpsimd",
                        wdsb.ap().rearrange("p (s c m) -> p s c m",
                                            s=2, c=4)[:, slot],
                        wd_i.ap().rearrange("(n c p) m -> n p c m",
                                            n=nexp, p=128)[en],
                        after=[wg_free[slot]], inc="ldw")
            wg_last = []
            for stq in range(2):
                c0 = stq * 256
                if en < 16:
                    tr1 = p.op("tensor", lambda e, en=en, stq=stq:
                        e.transpose(p.psum_banks[4][0:1, 0:128],
                        cmbt4[:, (2 * stq) * 16 + en:(2 * stq) * 16 + en + 1],
                        eye_f[:, :]),
                        after=[gt_bar, tid, p.psum_last_read[4]], inc="mm")
                    cp1 = p.op("vector", lambda e: e.tensor_copy(
                        crow[0:1, 0:128], p.psum_banks[4][0:1, 0:128]),
                        after=[tr1, crow_free], inc="ev")
                    tr2 = p.op("tensor", lambda e, en=en, stq=stq:
                        e.transpose(p.psum_banks[4][0:1, 128:256],
                        cmbt4[:, (2 * stq + 1) * 16 + en:
                              (2 * stq + 1) * 16 + en + 1],
                        eye_f[:, :]), after=[gt_bar, cp1], inc="mm")
                    cp2 = p.op("vector", lambda e: e.tensor_copy(
                        crow[0:1, 128:256], p.psum_banks[4][0:1, 128:256]),
                        after=[tr2], inc="ev")
                    cb1 = p.op("tensor", lambda e: e.matmul(
                        p.psum_banks[4][:, 256:512], p.ones_row128[0:1, :],
                        crow[0:1, :], start=True, stop=True),
                        after=[cp2], inc="mm")
                    crow_free = cb1
                    cb2 = p.op("vector", lambda e: e.tensor_copy(
                        cbsb[:, :], p.psum_banks[4][:, 256:512]),
                        after=[cb1], inc="ev")
                    p.psum_last_read[4] = cb2
                else:
                    cb2 = None
                mmg_toks, mmu_toks = [], []
                for mc in range(4):
                    bg = 0 if mc < 2 else 1
                    bu = 2 if mc < 2 else 3
                    col = (mc % 2) * 256
                    mmg = mmu = None
                    for kc in range(8):
                        mmg = p.op("tensor", lambda e, mc=mc, kc=kc, bg=bg,
                                   col=col, slot=slot, c0=c0: e.matmul(
                            p.psum_banks[bg][:, col:col + 256],
                            wgsb[:, slot * 4096 + kc * 512 + mc * 128:
                                 slot * 4096 + kc * 512 + mc * 128 + 128],
                            x4h[:, kc * 512 + c0:kc * 512 + c0 + 256],
                            start=(kc == 0), stop=(kc == 7)),
                            after=([lwg, ln_t, p.psum_last_read[bg]])
                            if kc == 0 else (),
                            inc="mm" if kc == 7 else None)
                        mmu = p.op("tensor", lambda e, mc=mc, kc=kc, bu=bu,
                                   col=col, slot=slot, c0=c0: e.matmul(
                            p.psum_banks[bu][:, col:col + 256],
                            wusb[:, slot * 4096 + kc * 512 + mc * 128:
                                 slot * 4096 + kc * 512 + mc * 128 + 128],
                            x4h[:, kc * 512 + c0:kc * 512 + c0 + 256],
                            start=(kc == 0), stop=(kc == 7)),
                            after=([lwu, p.psum_last_read[bu]])
                            if kc == 0 else (),
                            inc="mm" if kc == 7 else None)
                    mmg_toks.append(mmg)
                    mmu_toks.append(mmu)
                    wg_last.extend([mmg, mmu])
                ev_toks = []
                for mc in range(4):
                    bg = 0 if mc < 2 else 1
                    bu = 2 if mc < 2 else 3
                    col = (mc % 2) * 256
                    pair = (mc // 2) * 2 + 1
                    ev1 = p.op("scalar", lambda e, mc=mc, bg=bg, col=col:
                        e.activation(glb[:, mc * 256:(mc + 1) * 256],
                        p.psum_banks[bg][:, col:col + 256], AF.Silu),
                        after=[mmg_toks[pair], gt_bar_s, prev_dbar],
                        inc="sq")
                    p.psum_last_read[bg] = ev1
                    if en < 16:
                        ev2 = p.op("vector", lambda e, mc=mc, bu=bu, col=col:
                            e.tensor_mul(ubuf[:, mc * 256:(mc + 1) * 256],
                            p.psum_banks[bu][:, col:col + 256], cbsb[:, :]),
                            after=[mmu_toks[pair], cb2], inc="ev")
                    else:
                        ev2 = p.op("vector", lambda e, mc=mc, bu=bu, col=col:
                            e.tensor_copy(ubuf[:, mc * 256:(mc + 1) * 256],
                            p.psum_banks[bu][:, col:col + 256]),
                            after=[mmu_toks[pair]], inc="ev")
                    p.psum_last_read[bu] = ev2
                    ev3 = p.op("vector", lambda e, mc=mc: e.tensor_mul(
                        glb[:, mc * 256:(mc + 1) * 256],
                        glb[:, mc * 256:(mc + 1) * 256],
                        ubuf[:, mc * 256:(mc + 1) * 256]),
                        after=[ev1, ev2], inc="ev")
                    ev_toks.append(ev3)
                dn_mm = []
                for oc in range(8):
                    bank = 4 + oc // 2
                    col = (oc % 2) * 256
                    mm = None
                    for mc in range(4):
                        mm = p.op("tensor", lambda e, oc=oc, mc=mc, bank=bank,
                                  col=col, slot=slot: e.matmul(
                            p.psum_banks[bank][:, col:col + 256],
                            wdsb[:, slot * 4096 + mc * 1024 + oc * 128:
                                 slot * 4096 + mc * 1024 + oc * 128 + 128],
                            glb[:, mc * 256:(mc + 1) * 256],
                            start=(mc == 0), stop=(mc == 3)),
                            after=([lwd] + ev_toks +
                                   [dn_last.get((stq, oc))])
                            if mc == 0 else (),
                            inc="mm" if mc == 3 else None)
                    dn_mm.append(mm)
                    wg_last.append(mm)
                for oc in range(8):
                    bank = 4 + oc // 2
                    col = (oc % 2) * 256
                    pair = (oc // 2) * 2 + 1
                    ad = p.op("vector", lambda e, oc=oc, stq=stq, bank=bank,
                              col=col: e.tensor_add(
                        mscr[:, oc * 512 + stq * 256:oc * 512 + stq * 256
                             + 256],
                        mscr[:, oc * 512 + stq * 256:oc * 512 + stq * 256
                             + 256],
                        p.psum_banks[bank][:, col:col + 256]),
                        after=[dn_mm[pair], acc_toks[(stq, oc)]], inc="ev")
                    acc_toks[(stq, oc)] = ad
                    dn_last[(stq, oc)] = ad
                    p.psum_last_read[bank] = ad
                prev_dbar = dn_mm[7]
            wg_free[slot] = p.op("tensor", lambda e: None, after=wg_last,
                                 inc="mm")
        if dbg and tt == 0:
            dbg_dmas.append(p.dma("sync", moe7_o.ap(), mscr[:, :],
                                  after=list(acc_toks.values()), inc="std"))
        fin_toks = []
        for stq in range(2):
            for oc in range(8):
                fo = p.op("vector", lambda e, oc=oc, stq=stq: e.tensor_add(
                    outsm[:, (oc % 2) * 256:(oc % 2) * 256 + 256],
                    mscr[:, oc * 512 + stq * 256:oc * 512 + stq * 256 + 256],
                    xc_t[:, oc * 512 + stq * 256:oc * 512 + stq * 256
                         + 256]),
                    after=[acc_toks[(stq, oc)],
                           fin_toks[-2] if len(fin_toks) >= 2 else None],
                    inc="ev")
                fd = p.dma("sync", fm_ap(out_o)
                           [:, oc, t0 + stq * 256:t0 + stq * 256 + 256],
                           outsm[:, (oc % 2) * 256:(oc % 2) * 256 + 256],
                           after=[fo], inc="sto")
                fin_toks.append(fd)
        out_dmas.extend(fin_toks)
        tile_done = p.op("vector", lambda e: None, after=fin_toks +
                         [wg_free[0], wg_free[1]], inc="ev")
    p.barrier_wait("sync", out_dmas + dbg_dmas + [tile_done])
    return p.build()



def host_rotary_tables(core):
    d = np.arange(64)
    half = (d % 32) * 2.0
    inv_freq = 1.0 / 10000.0 ** (half / 64.0)
    base = (half + 0.4 * 64) / (1.4 * 64)
    t = np.arange(core * 2048 - HALO, core * 2048 + 2048, dtype=np.float64)
    freqs = t[:, None] * inv_freq[None, :]
    power = (t - 16384 // 2) / 256.0
    scale = base[None, :] ** power[:, None]
    cos, sin = np.cos(freqs), np.sin(freqs)
    tabs = np.zeros((128, 4 * TEXT), np.float32)
    for g, tab in enumerate([cos * scale, sin * scale, cos / scale, sin / scale]):
        tt = tab.T.astype(np.float32)
        tabs[0:64, g * TEXT:(g + 1) * TEXT] = tt
        tabs[64:128, g * TEXT:(g + 1) * TEXT] = tt
    return tabs.astype(BF)


def host_attn_masks(core):
    m = np.zeros((128, 17 * 640), np.float32)
    for mi in range(17):
        ci = mi - 1
        i_g = core * 2048 + ci * 128 + np.arange(128)[:, None]
        jparts = []
        for jj in range(5):
            jc = ci - 4 + jj
            j_g = core * 2048 + jc * 128 + np.arange(128)[None, :]
            valid = (j_g >= 0) & (j_g >= i_g - 512) & (j_g <= i_g)
            jparts.append(np.where(valid, 0.0, -1e30).astype(np.float32))
        m[:, mi * 640:mi * 640 + 512] = np.concatenate(jparts[:4], axis=1)
        m[:, mi * 640 + 512:(mi + 1) * 640] = jparts[4]
    return m.astype(BF)


def host_ffn_inmaps(core, layer, s_ext, ins, xmid_ext=None, xresid=None):
    wqkv = ins["L0_ffn_wqkv"][layer]
    qs = np.zeros((128, 2), np.float32)
    qs[:, 0] = np.tile(ins["L0_ffn_q_scale"][layer], 2)
    qs[:, 1] = np.tile(ins["L0_ffn_k_scale"][layer], 2)
    a = {
        "s_ext": s_ext,
        "wqkv": np.ascontiguousarray(wqkv),
        "wo": np.ascontiguousarray(ins["L0_ffn_wo"][layer]),
        "qs": qs,
        "ident128": np.eye(128, dtype=np.float32),
        "rtabs": host_rotary_tables(core),
        "amask": host_attn_masks(core),
    }
    b1 = ins["L0_ffn_b1"][layer]
    b1a = np.zeros((22, 128), np.float32); b1a.ravel()[:2730] = b1[:2730]
    b1g = np.zeros((22, 128), np.float32); b1g.ravel()[:2730] = b1[2730:]
    b = {
        "w1": np.ascontiguousarray(ins["L0_ffn_w1"][layer]),
        "b1a": b1a, "b1g": b1g,
        "w2": np.ascontiguousarray(ins["L0_ffn_w2"][layer]),
        "b2": ins["L0_ffn_b2"][layer].reshape(8, 128).copy(),
        "lng": ins["L0_ffn_ln_g"][layer].reshape(8, 128).copy(),
        "lnb": ins["L0_ffn_ln_b"][layer].reshape(8, 128).copy(),
    }
    if xmid_ext is not None:
        b["xmid_ext"] = xmid_ext
    if xresid is not None:
        b["x_resid"] = xresid
    return a, b


def assemble_ext(full_T, core, halo):
    lo = core * 2048 - halo
    out = np.zeros((full_T.shape[0], halo + 2048), full_T.dtype)
    src_lo = max(lo, 0)
    out[:, src_lo - lo:] = full_T[:, src_lo: core * 2048 + 2048]
    return out



_NC_CACHE = {}


def _get_nc(name, builder):
    if name not in _NC_CACHE:
        _NC_CACHE[name] = builder()
    return _NC_CACHE[name]


def run_launch(name, builder, in_maps, trace=False):
    nc = _get_nc(name, builder)
    res = run_bass_kernel_spmd(nc, in_maps, core_ids=list(range(NC)),
                               trace=trace)
    return res.results


def kernel(**inputs):
    ins = {k: np.asarray(v) for k, v in inputs.items()}
    x = ins["L0_attn_x"][0]
    W = ins["L0_attn_W_attn"]
    eye64 = np.eye(64, dtype=np.float32)
    eye128 = np.eye(128, dtype=np.float32)

    l1_maps = [{
        "xT": np.ascontiguousarray(x[c * 2048:(c + 1) * 2048].T),
        "Wqk": np.ascontiguousarray(W[:, :2048]),
        "Wv": np.ascontiguousarray(W[:, 2048:]),
    } for c in range(NC)]
    r1 = run_launch("L1", build_L1, l1_maps)
    ql_all = np.ascontiguousarray(np.concatenate(
        [r1[c]["qlkl"][:, 0:8] for c in range(NC)], 1))
    kl_all = np.ascontiguousarray(np.concatenate(
        [r1[c]["qlkl"][:, 8:16] for c in range(NC)], 1))

    l2_maps = [{"kT": r1[c]["kT"], "vtok": r1[c]["vtok"], "ql_all": ql_all}
               for c in range(NC)]
    r2 = run_launch("L2", build_L2, l2_maps)
    nv_all = np.ascontiguousarray(np.concatenate(
        [r2[c]["nv"] for c in range(NC)], 1))

    Wp = np.ascontiguousarray(ins["L0_attn_W_proj"])
    l3_maps = [{"qT": r1[c]["qT"], "ql_all": ql_all, "kl_all": kl_all,
                "nv_all": nv_all, "Wproj": Wp, "xT": l1_maps[c]["xT"],
                "ident": eye64} for c in range(NC)]
    r3 = run_launch("L3", build_L3, l3_maps)
    xa_full = np.concatenate([r3[c]["xa"] for c in range(NC)], 1)

    l4a_maps = []
    for c in range(NC):
        a, _ = host_ffn_inmaps(
            c, 0, np.ascontiguousarray(assemble_ext(xa_full, c, HALO)), ins)
        l4a_maps.append(a)
    r4a = run_launch("L4a", lambda: build_ffn_attn(True), l4a_maps)
    xm_full = np.concatenate([r4a[c]["xmid"][:, 128:] for c in range(NC)], 1)
    l4b_maps = []
    for c in range(NC):
        _, b = host_ffn_inmaps(c, 0, None, ins, xmid_ext=np.ascontiguousarray(
            assemble_ext(xm_full, c, 1)))
        l4b_maps.append(b)
    r4b = run_launch("L4b", lambda: build_ffn_ffn(False), l4b_maps)
    x1_full = np.concatenate([r4b[c]["s_out"] for c in range(NC)], 1)

    l5a_maps = []
    for c in range(NC):
        a, _ = host_ffn_inmaps(
            c, 1, np.ascontiguousarray(assemble_ext(x1_full, c, HALO)), ins)
        l5a_maps.append(a)
    r5a = run_launch("L5a", lambda: build_ffn_attn(False), l5a_maps)
    xm2_full = np.concatenate([r5a[c]["xmid"][:, 128:] for c in range(NC)], 1)
    l5b_maps = []
    for c in range(NC):
        _, b = host_ffn_inmaps(c, 1, None, ins, xmid_ext=np.ascontiguousarray(
            assemble_ext(xm2_full, c, 1)),
            xresid=np.ascontiguousarray(xa_full[:, c * 2048:(c + 1) * 2048]))
        l5b_maps.append(b)
    r5b = run_launch("L5b", lambda: build_ffn_ffn(True), l5b_maps)
    xb_full = np.concatenate([r5b[c]["s_out"] for c in range(NC)], 1)

    Wip = ins["L1_attn_in_proj_w"]
    wN = np.zeros((4352, 1024), np.float32)
    wN[:4240] = Wip
    wT = np.ascontiguousarray(wN.T)
    cw = np.ascontiguousarray(ins["L1_attn_conv_w"].reshape(17, 128, 4)
                              .transpose(1, 0, 2).reshape(128, 68))
    cb = np.ascontiguousarray(ins["L1_attn_conv_b"].reshape(17, 128).T)
    tri = np.where(np.arange(128)[None, :] >= np.arange(128)[:, None],
                   0.0, -1e30).astype(np.float32)
    l6_maps = [{
        "xb_ext": np.ascontiguousarray(assemble_ext(xb_full, c, 128)),
        "inprojT": wT, "inprojN": wN, "convw": cw, "convb": cb,
        "dtb": ins["L1_attn_dt_bias"].reshape(16, 1).astype(np.float32),
        "alog": ins["L1_attn_A_log"].reshape(16, 1).astype(np.float32),
        "dval": ins["L1_attn_D"].reshape(1, 16).astype(np.float32),
        "xnw": ins["L1_attn_x_norm_w"].reshape(8, 128).astype(np.float32),
        "xnb": ins["L1_attn_x_norm_b"].reshape(8, 128).astype(np.float32),
        "ident128": eye128, "trimask": tri,
    } for c in range(NC)]
    r6 = run_launch("L6", build_mamba_local, l6_maps)

    wg16 = ins["L1_ffn_w_gate"].reshape(16 * 1024, 512)
    wu16 = ins["L1_ffn_w_up"].reshape(16 * 1024, 512)
    wd16 = ins["L1_ffn_w_down"].reshape(16 * 512, 1024)
    swg, swu, swd = (ins["L1_ffn_sw_gate"], ins["L1_ffn_sw_up"],
                     ins["L1_ffn_sw_down"])
    wg_all = np.ascontiguousarray(np.concatenate(
        [wg16, swg[:, 0:512], swg[:, 512:1024]], 0).astype(np.float32))
    wu_all = np.ascontiguousarray(np.concatenate(
        [wu16, swu[:, 0:512], swu[:, 512:1024]], 0).astype(np.float32))
    wd_all = np.ascontiguousarray(np.concatenate(
        [wd16, swd[0:512], swd[512:1024]], 0).astype(np.float32))
    rmsw = np.ascontiguousarray(
        ins["L1_attn_rms_w"].reshape(16, 128).T.astype(np.float32))
    opT = np.ascontiguousarray(ins["L1_attn_out_proj_w"].T.astype(np.float32))
    opN = np.ascontiguousarray(ins["L1_attn_out_proj_w"].astype(np.float32))
    gateW = np.ascontiguousarray(ins["L1_ffn_gate_kernel"].astype(np.float32))
    gateB = ins["L1_ffn_gate_bias"].reshape(1, 16).astype(np.float32)
    l7_maps = []
    for c in range(NC):
        S_prev = np.zeros((7 * 64, 2048), np.float32)
        A_prev = np.zeros((1, 112), np.float32)
        for j in range(c):
            S_prev[j * 64:(j + 1) * 64] = r6[j]["S_core"]
            A_prev[0, j * 16:(j + 1) * 16] = r6[j]["A_core"][:, 0]
        l7_maps.append({
            "y_locT": r6[c]["y_locT"], "chT": r6[c]["chT"],
            "zT": r6[c]["zT"], "S_prev": S_prev, "A_prev": A_prev,
            "xbT": np.ascontiguousarray(xb_full[:, c * 2048:(c + 1) * 2048]),
            "rmsw": rmsw, "outprojT": opT, "outprojN": opN,
            "gateW": gateW, "gateB": gateB,
            "wg_all": wg_all, "wu_all": wu_all, "wd_all": wd_all,
            "ident128": eye128,
        })
    r7 = run_launch("L7", build_final, l7_maps)

    out = np.empty((16384, 1024), np.float32)
    for c in range(NC):
        out[c * 2048:(c + 1) * 2048] = r7[c]["out"].T
    return out.reshape(1, 16384, 1024)
